# version 12
# baseline (speedup 1.0000x reference)
import math
import numpy as np
import ml_dtypes
import concourse.bass as bass
import concourse.mybir as mybir
from concourse.bass_utils import run_bass_kernel_spmd

F32 = mybir.dt.float32
BF16 = mybir.dt.bfloat16
AF = mybir.ActivationFunctionType
ALU = mybir.AluOpType
AX = mybir.AxisListType

D = 2048
NH = 16
DK = 128
CH = 128
IF = 768
NIC = IF // 128
NDCH = D // 128
PW = 512

ENGINES = ("pe", "act", "dve", "pool", "sp")


class OpHandle:
    __slots__ = ("key", "cum")
    def __init__(self, key, cum):
        self.key = key
        self.cum = cum


class Sched:

    def __init__(self, nc):
        self.nc = nc
        self.items = {e: [] for e in ENGINES}
        self.cum = {}
        self.last_compute = {}
        self.last_on_key = {}
        self.n_waits = 0
        self.n_ops = 0

    def add(self, engine, fn, deps=(), dsem=None, inc=None, chain=True):
        if dsem is None:
            key = ("c", engine)
            inc = 1 if inc is None else inc
        else:
            key = ("d", dsem)
            inc = 16 if inc is None else inc
        deps = [d for d in deps if d is not None]
        if dsem is None and engine in ("act", "dve", "pool") and chain:
            prev = self.last_compute.get(engine)
            if prev is not None:
                deps.append(prev)
        self.cum[key] = self.cum.get(key, 0) + inc
        h = OpHandle(key, self.cum[key])
        if dsem is None and engine in ("act", "dve", "pool"):
            self.last_compute[engine] = h
        self.last_on_key[key] = h
        self.items[engine].append((fn, tuple(deps), inc, key))
        self.n_ops += 1
        return h

    def barrier(self):
        deps = list(self.last_on_key.values())
        return self.add("pool", lambda g: g.memset(self.nc_dummy[:], 0.0), deps=deps)

    def build(self):
        nc = self.nc
        sems = {}
        ctxs = []
        for key in self.cum:
            cm = nc.semaphore(f"S_{key[0]}_{key[1]}")
            sems[key] = cm.__enter__()
            ctxs.append(cm)
        blk_cm = nc.Block()
        block = blk_cm.__enter__()
        ctxs.append(blk_cm)

        def make_body(e):
            eng = {"pe": nc.tensor, "act": nc.scalar, "dve": nc.vector,
                   "pool": nc.gpsimd, "sp": nc.sync}[e]
            items = self.items[e]
            sched = self

            def body(_):
                waited = {}
                for fn, deps, inc, key in items:
                    need = {}
                    for dd in deps:
                        if dd.key == key == ("c", "pe"):
                            continue
                        if dd.cum > need.get(dd.key, 0):
                            need[dd.key] = dd.cum
                    for k, v in need.items():
                        if v > waited.get(k, 0):
                            eng.wait_ge(sems[k], v)
                            waited[k] = v
                            sched.n_waits += 1
                    instr = fn(eng)
                    instr.then_inc(sems[key], inc)
            return body

        for e in ENGINES:
            if not self.items[e]:
                continue
            dec = {"pe": block.tensor, "act": block.scalar, "dve": block.vector,
                   "pool": block.gpsimd, "sp": block.sync}[e]
            dec(make_body(e))
        return ctxs


def build_program(NP=16, NM=16, NE=32, EG=2, stop_after=None):
    TP, TM = NP * CH, NM * CH
    NTG = (TP + TM) // PW
    NTGM = TM // PW
    NCA = NP + NM

    nc = bass.Bass("TRN2", target_bir_lowering=False)

    xp = nc.dram_tensor("xp", [TP, D], F32, kind="ExternalInput").ap()
    xm = nc.dram_tensor("xm", [TM, D], F32, kind="ExternalInput").ap()
    wq_d = nc.dram_tensor("wq", [D, D], F32, kind="ExternalInput").ap()
    wk_d = nc.dram_tensor("wk", [D, D], F32, kind="ExternalInput").ap()
    wv_d = nc.dram_tensor("wv", [D, D], F32, kind="ExternalInput").ap()
    wg_d = nc.dram_tensor("wg", [D, D], F32, kind="ExternalInput").ap()
    wo_d = nc.dram_tensor("wo", [D, D], F32, kind="ExternalInput").ap()
    rwt_d = nc.dram_tensor("rwt", [D, NE], F32, kind="ExternalInput").ap()
    gwt_d = nc.dram_tensor("gwt", [NE, D, IF], F32, kind="ExternalInput").ap()
    uwt_d = nc.dram_tensor("uwt", [NE, D, IF], F32, kind="ExternalInput").ap()
    dwt_d = nc.dram_tensor("dwt", [NE, IF, D], F32, kind="ExternalInput").ap()
    flag_d = nc.dram_tensor("flag", [128, 1], F32, kind="ExternalInput").ap()
    ident_d = nc.dram_tensor("ident", [128, 128], F32, kind="ExternalInput").ap()
    mask_d = nc.dram_tensor("maskst", [128, 512], F32, kind="ExternalInput").ap()
    sel_d = nc.dram_tensor("sel", [NE, NE * 128], F32, kind="ExternalInput").ap()
    out_d = nc.dram_tensor("out", [TM, D], F32, kind="ExternalOutput").ap()

    xn_scr = nc.dram_tensor("xn_scr", [NTG, 128, NDCH * PW], BF16, kind="Internal").ap()
    k_scr = nc.dram_tensor("k_scr", [NCA, 128, NH * CH], BF16, kind="Internal").ap()
    g_scr = nc.dram_tensor("g_scr", [NCA, 128, NH * CH], BF16, kind="Internal").ap()
    q_scr = nc.dram_tensor("q_scr", [NM, 128, NH * CH], BF16, kind="Internal").ap()
    v_scr = nc.dram_tensor("v_scr", [NCA, CH, D], BF16, kind="Internal").ap()
    h_scr = nc.dram_tensor("h_scr", [NE, NM, IF, CH], BF16, kind="Internal").ap()
    at_scr = nc.dram_tensor("at_scr", [NM, CH, D], F32, kind="Internal").ap()

    s = Sched(nc)
    ctxs = []

    def sb(name, shape, dtype):
        cm = nc.sbuf_tensor(name, shape, dtype)
        t = cm.__enter__()
        ctxs.append(cm)
        return t

    def ps(name, shape, dtype=F32):
        cm = nc.psum_tensor(name, shape, dtype)
        t = cm.__enter__()
        ctxs.append(cm)
        return t

    bank = [ps(f"bank{i}", [128, PW], F32) for i in range(7)]
    bank_t = ps("bank_t", [128, 1024], BF16)
    bank6_bf = bank[6].bitcast(BF16)

    dummy = sb("dummy_sb", [1, 2], F32)
    s.nc_dummy = dummy
    cst = sb("cst", [128, 8], F32)
    ident_bf = sb("ident_bf", [128, 128], BF16)
    mask_bf = sb("mask_bf", [128, 512], BF16)
    ones_f = sb("ones_f", [128, 128], F32)
    S_a = sb("S_a", [128, NH * DK], F32)
    S_b = sb("S_b", [128, NH * DK], F32)
    S_bf = sb("S_bf", [128, NH * DK], BF16)
    lns = sb("lns", [128, 4, 8], F32)
    tiny = sb("tiny", [128, 2, NH, 2], F32)
    rw_sb = sb("rw_sb", [128, NDCH, NE], BF16)
    w_all = sb("w_all", [NE, TM], BF16)
    sel_sb = sb("sel_sb", [NE, NE, 128], BF16)
    logit_t = sb("logit_t", [128, NE], F32)
    rmask = sb("rmask", [128, NE], F32)
    top8 = sb("top8", [128, 8], F32)
    wexp = sb("wexp", [128, NE], F32)
    wexp_bf = sb("wexp_bf", [128, NE], BF16)
    rtiny = sb("rtiny", [128, 2], F32)

    OVB = 176256
    arena32 = sb("arena", [128, OVB // 4], F32)
    arena16 = arena32.bitcast(BF16)

    def vw(off, shape, dtype):
        assert off % 64 == 0
        n = 1
        for d in shape[1:]:
            n *= d
        if dtype == F32:
            ap = arena32.ap()[0:shape[0], off // 4: off // 4 + n]
        else:
            ap = arena16.ap()[0:shape[0], off // 2: off // 2 + n]
        if len(shape) == 3:
            ap = ap.rearrange("p (a b) -> p a b", a=shape[1])
        elif len(shape) == 4:
            ap = ap.rearrange("p (a b c) -> p a b c", a=shape[1], b=shape[2])
        assert off + n * (4 if dtype == F32 else 2) <= OVB
        return ap


    def _stop_emit():
        outs = []
        for tt in range(NM):
            outs.append(s.add("sp", lambda g, tt=tt: g.dma_start(
                out=out_d[tt * 128:(tt + 1) * 128, :],
                in_=arena32.ap()[:, 0:D]), deps=[s.barrier()], dsem="outw"))
        s.add("sp", lambda g: g.nop(), deps=outs)
        built = s.build()
        for cm in reversed(built):
            cm.__exit__(None, None, None)
        for cm in reversed(ctxs):
            cm.__exit__(None, None, None)
        return nc, s
    xf = [vw(i * 8192, [128, D], F32) for i in range(4)]
    xn_tm = [vw(32768 + i * 4096, [128, D], BF16) for i in range(4)]
    xng = [vw(49152 + i * 16384, [128, NDCH, PW], BF16) for i in range(4)]
    wbuf = [vw(114688, [128, NDCH, 512], BF16), vw(131072, [128, NDCH, 512], BF16)]
    ev = [vw(147456 + i * 1024, [128, PW], BF16) for i in range(4)]
    gzmin = vw(151552, [128, PW], F32)
    gtmp = vw(153600, [128, PW], F32)
    sqd = vw(155648, [128, D], BF16)

    kt_l = [vw(0, [128, NH, CH], BF16), vw(4096, [128, NH, CH], BF16)]
    gt_l = [vw(8192, [128, NH, CH], BF16), vw(12288, [128, NH, CH], BF16)]
    qt_l = [vw(16384, [128, NH, CH], BF16), vw(20480, [128, NH, CH], BF16)]
    vt_l = [vw(24576, [128, D], BF16), vw(28672, [128, D], BF16),
            vw(167936, [128, D], BF16)]
    cum_b = vw(32768, [128, NH, CH], F32)
    lq_all = [vw(40960, [128, NH, CH], BF16), vw(45056, [128, NH, CH], BF16)]
    kd_all = [vw(49152, [128, NH, CH], BF16), vw(53248, [128, NH, CH], BF16)]
    qt_all = [vw(57344, [128, NH, CH], BF16), vw(61440, [128, NH, CH], BF16)]
    ktq_all = [vw(65536, [128, NH, CH], BF16), vw(69632, [128, NH, CH], BF16)]
    ke_all = [vw(73728, [128, NH, CH], BF16), vw(77824, [128, NH, CH], BF16)]
    atm4 = vw(81920, [128, 2, 4, CH], BF16)
    ket4 = vw(83968, [128, 2, 4, CH], BF16)
    o_fm = [vw(86016, [128, NH, CH], BF16), vw(90112, [128, NH, CH], BF16)]
    attn_t = [vw(94208, [128, D], F32)]
    wo_sb = vw(102400, [128, NDCH, D], BF16)

    xng_m = [vw(0, [128, NDCH, PW], BF16), vw(16384, [128, NDCH, PW], BF16)]
    guw = [vw(32768 + i * 24576, [128, NDCH, IF], BF16) for i in range(4)]
    sg_b = vw(131072, [128, NIC, PW], BF16)
    h_b = vw(137280, [128, NIC, PW], BF16)
    w_bc = [vw(143488, [128, TM], BF16), vw(151680, [128, TM], BF16)]

    NMH = NM // 2
    acc = vw(0, [128, NMH, D], F32)
    dwb = [vw(NMH * D * 4 + i * (NIC * D * 2), [128, NIC, D], BF16)
           for i in range(2 * EG)]
    _dw_end = NMH * D * 4 + 2 * EG * NIC * D * 2
    h_in = [vw(_dw_end + i * (NIC * CH * 2), [128, NIC, CH], BF16)
            for i in range(4 * EG)]

    c0 = s.add("pool", lambda g: g.memset(cst[:, 0:1], 1e-5))
    s.add("pool", lambda g: g.memset(cst[:, 1:2], 1.0))
    s.add("pool", lambda g: g.memset(cst[:, 2:3], 1.0 / 16.0))
    s.add("pool", lambda g: g.memset(cst[:, 3:4], -1.0 / 16.0))
    s.add("pool", lambda g: g.memset(cst[:, 4:5], -0.5 * math.log(DK)))
    s.add("pool", lambda g: g.memset(ones_f[:], 1.0))
    init_zs = s.add("pool", lambda g: g.memset(S_a[:], 0.0))
    l_flag = s.add("sp", lambda g: g.dma_start(out=cst[:, 5:6], in_=flag_d), dsem="ini")
    l_id = s.add("pool", lambda g: g.dma_start(out=ident_bf[:], in_=ident_d), dsem="in_id")
    l_mask = s.add("pool", lambda g: g.dma_start(out=mask_bf[:], in_=mask_d), dsem="in_mk")
    l_sel = s.add("pool", lambda g: g.dma_start(
        out=sel_sb[:], in_=sel_d.rearrange("p (e t) -> p e t", e=NE)), dsem="in_sel")
    l_rw = s.add("pool", lambda g: g.dma_start(
        out=rw_sb[:], in_=rwt_d.rearrange("(c p) e -> p c e", p=128)), dsem="in_rw")
    init_c = s.add("pool", lambda g: g.memset(dummy[:], 0.0), deps=[c0, init_zs])

    if stop_after == "init":
        return _stop_emit()
    NTC = (TP + TM) // 128
    xf_gate = [None] * 4
    xng_gate = [None] * 4
    xn_tm_gate = [None] * 4
    slot_evac = [None] * 4
    a0_group_write = [None] * NTG
    for tc in range(NTC):
        sl = tc % 4
        grp = tc // 4
        gsl = grp % 4
        src_ap = xp[tc * 128:(tc + 1) * 128, :] if tc < NP else \
            xm[(tc - NP) * 128:(tc - NP + 1) * 128, :]
        lx = s.add("sp", lambda g, src_ap=src_ap, sl=sl: g.dma_start(
            out=xf[sl][:], in_=src_ap), deps=[xf_gate[sl]], dsem=f"a0x{sl}")
        m_sum = s.add("dve", lambda v, sl=sl: v.reduce_sum(
            lns[:, sl, 0:1], xf[sl][:], axis=AX.X), deps=[lx])
        m_sq = s.add("act", lambda a, sl=sl: a.activation(
            sqd[:], xf[sl][:], AF.Square, accum_out=lns[:, sl, 1:2]),
            deps=[lx, m_sum])
        m_mean = s.add("dve", lambda v, sl=sl: v.tensor_scalar(
            lns[:, sl, 2:3], lns[:, sl, 0:1], 1.0 / D, None, op0=ALU.mult))
        m_m2 = s.add("act", lambda a, sl=sl: a.activation(
            lns[:, sl, 3:4], lns[:, sl, 2:3], AF.Square), deps=[m_mean])
        m_t = s.add("dve", lambda v, sl=sl: v.tensor_scalar(
            lns[:, sl, 4:5], lns[:, sl, 1:2], 1.0 / D, None, op0=ALU.mult),
            deps=[m_sq])
        m_var = s.add("dve", lambda v, sl=sl: v.tensor_sub(
            lns[:, sl, 5:6], lns[:, sl, 4:5], lns[:, sl, 3:4]), deps=[m_m2])
        m_srt = s.add("act", lambda a, sl=sl: a.activation(
            lns[:, sl, 6:7], lns[:, sl, 5:6], AF.Sqrt, bias=cst[:, 0:1]),
            deps=[m_var, init_c])
        m_rstd = s.add("dve", lambda v, sl=sl: v.reciprocal(
            lns[:, sl, 7:8], lns[:, sl, 6:7]), deps=[m_srt])
        m_xn = s.add("dve", lambda v, sl=sl: v.tensor_scalar(
            xn_tm[sl][:], xf[sl][:], lns[:, sl, 2:3], lns[:, sl, 7:8],
            op0=ALU.subtract, op1=ALU.mult), deps=[xn_tm_gate[sl]])
        xf_gate[sl] = m_xn
        tws = []
        tci = tc % 4
        for bq in range(4):
            bslot = (tc * 4 + bq) % 2
            area = bank_t[:, bslot * 512:bslot * 512 + 512]
            last_tp = None
            for j in range(4):
                dch = bq * 4 + j
                def tfn(pe, sl=sl, dch=dch, area=area, j=j):
                    return pe.transpose(
                        area[:, j * 128:(j + 1) * 128],
                        xn_tm[sl][:, dch * 128:(dch + 1) * 128], ident_bf[:])
                last_tp = s.add("pe", tfn,
                                deps=[m_xn, l_id, slot_evac[0], slot_evac[1],
                                      last_tp])
            e_t = s.add("act", lambda a, gsl=gsl, bq=bq, area=area, tci=tci: a.copy(
                xng[gsl][:, bq * 4:(bq + 1) * 4, tci * 128:(tci + 1) * 128],
                area[:]),
                deps=[last_tp, xng_gate[gsl] if tci == 0 else None])
            slot_evac[bslot] = e_t
            tws.append(e_t)
            if bq == 3:
                xn_tm_gate[sl] = last_tp
        if tci == 3:
            wxn = s.add("sp", lambda g, gsl=gsl, grp=grp: g.dma_start(
                out=xn_scr[grp, :, :],
                in_=xng[gsl][:].rearrange("p a b -> p (a b)")), deps=tws,
                dsem=f"a0w{grp % 2}")
            xng_gate[gsl] = wxn
            a0_group_write[grp] = wxn

    xng_rd_gate = list(xng_gate)
    xng_pe_gate = [None, None, None, None]
    wbuf_pe_gate = [None, None]
    bank_rd = [None] * 7
    bank_wr = [None] * 7
    ev_gate = [None] * 4
    gz_dve_gate = None
    sq_dve_gate = None
    scr_w_last = {}

    def a_load_xng(tg, sl):
        dep = [xng_rd_gate[sl], xng_pe_gate[sl],
               a0_group_write[-1], a0_group_write[-2]]
        h = s.add("sp", lambda g, tg=tg, sl=sl: g.dma_start(
            out=xng[sl][:].rearrange("p a b -> p (a b)"), in_=xn_scr[tg, :, :]),
            deps=dep, dsem=f"apx{sl}")
        xng_rd_gate[sl] = h
        return h

    def a_load_w(wsrc, qrt, slot):
        last = None
        for dch in range(NDCH):
            last = s.add("pool", lambda g, wsrc=wsrc, qrt=qrt, slot=slot, dch=dch:
                         g.dma_start(out=wbuf[slot][:, dch, :],
                                     in_=wsrc[dch * 128:(dch + 1) * 128,
                                              qrt * 512:(qrt + 1) * 512]),
                         deps=[wbuf_pe_gate[slot]] if dch == 0 else [],
                         dsem=f"apw{slot}")
        return last

    pass_list = [("k", wk_d, k_scr), ("g", wg_d, g_scr), ("v", wv_d, v_scr),
                 ("q", wq_d, q_scr)]
    bank_i = 0
    for pi, (pname, wsrc, scr) in enumerate(pass_list):
        ntg = NTGM if pname == "q" else NTG
        tg_off = TP // PW if pname == "q" else 0
        if stop_after == "pass_" + pname:
            return _stop_emit()
        for qrt in range(4):
            wslot = (pi * 4 + qrt) % 2
            lw = a_load_w(wsrc, qrt, wslot)
            w_readers = []
            for tg in range(ntg):
                xsl = tg % 4
                lx = a_load_xng(tg + tg_off, xsl)
                x_readers = []
                for fc in range(4):
                    bk = bank_i % 6
                    bank_i += 1
                    if pname != "v":
                        def mmf(pe, fc=fc, xsl=xsl, bk=bk, wslot=wslot):
                            r = None
                            for dch in range(NDCH):
                                r = pe.matmul(bank[bk][:],
                                              wbuf[wslot][:, dch, fc * 128:(fc + 1) * 128],
                                              xng[xsl][:, dch, :],
                                              start=(dch == 0), stop=(dch == NDCH - 1))
                            return r
                    else:
                        tci = fc
                        def mmf(pe, tci=tci, xsl=xsl, bk=bk, wslot=wslot):
                            r = None
                            for dch in range(NDCH):
                                r = pe.matmul(bank[bk][:],
                                              xng[xsl][:, dch, tci * 128:(tci + 1) * 128],
                                              wbuf[wslot][:, dch, :],
                                              start=(dch == 0), stop=(dch == NDCH - 1))
                            return r
                    mm = s.add("pe", mmf, deps=[lw, lx, bank_rd[bk]])
                    bank_wr[bk] = mm
                    x_readers.append(mm)
                    w_readers.append(mm)
                    evslot = (tg * 4 + fc) % 4
                    if pname == "g":
                        mop = s.add("dve", lambda v, bk=bk: v.tensor_scalar(
                            gzmin[:], bank[bk][:], 0.0, None, op0=ALU.min),
                            deps=[mm])
                        aop = s.add("act", lambda a, bk=bk: a.activation(
                            gtmp[:], bank[bk][:], AF.Abs), deps=[mm, mop, sq_dve_gate])
                        eop = s.add("act", lambda a: a.activation(
                            gtmp[:], gtmp[:], AF.Exp, scale=-1.0))
                        lop = s.add("act", lambda a: a.activation(
                            gtmp[:], gtmp[:], AF.Ln, bias=cst[:, 1:2]), deps=[init_c])
                        sub = s.add("dve", lambda v, evslot=evslot: v.tensor_sub(
                            ev[evslot][:], gzmin[:], gtmp[:]),
                            deps=[lop, ev_gate[evslot]])
                        sq_dve_gate = sub
                        bank_rd[bk] = sub
                        evh = sub
                    else:
                        evh = s.add("act", lambda a, bk=bk, evslot=evslot: a.copy(
                            ev[evslot][:], bank[bk][:]),
                            deps=[mm, ev_gate[evslot]])
                        bank_rd[bk] = evh
                    if pname != "v":
                        fcg = qrt * 4 + fc
                        wlast = None
                        for j in range(4):
                            cidx = (tg * PW + j * 128) // CH
                            def wfn(g, scr=scr, cidx=cidx, fcg=fcg, j=j, evslot=evslot):
                                return g.dma_start(
                                    out=scr[cidx, :, fcg * CH:(fcg + 1) * CH],
                                    in_=ev[evslot][:, j * 128:(j + 1) * 128])
                            wlast = s.add("act", wfn, deps=[evh],
                                          dsem=f"apo{evslot}")
                        ev_gate[evslot] = wlast
                    else:
                        tci = fc
                        cidx = tg * 4 + tci
                        fcol = qrt * 512
                        wlast = s.add("act", lambda g, cidx=cidx, fcol=fcol, evslot=evslot:
                                      g.dma_start(out=v_scr[cidx, :, fcol:fcol + 512],
                                                  in_=ev[evslot][:]),
                                      deps=[evh], dsem=f"apo{evslot}")
                        ev_gate[evslot] = wlast
                xng_pe_gate[xsl] = x_readers[-1]
            wbuf_pe_gate[wslot] = w_readers[-1]
    aproj_fin = [h for h in ev_gate if h is not None]

    if stop_after == "aproj":
        return _stop_emit()
    if stop_after == "aproj":
        return _stop_emit()
    gla = dict(
        k_gate=[None, None],
        v_gate=[None, None, None],
        act_last=None,
        pe_gate=[None, None],
        atm_gate=[None, None],
        ket_gate=[None, None],
        state_dep=init_zs,
        sbf_gate=None,
        tiny_gate=[None, None],
        tp_gate=[None, None],
        o_ev_last=None,
        ofm_pe_gate=None,
    )

    def gla_stage1(ci, main):
        sl = ci % 2
        lk = s.add("sp", lambda g, ci=ci, sl=sl: g.dma_start(
            out=kt_l[sl][:].rearrange("p a b -> p (a b)"), in_=k_scr[ci, :, :]),
            deps=[gla["k_gate"][sl]], dsem=f"glk{sl}")
        lg = s.add("sp", lambda g, ci=ci, sl=sl: g.dma_start(
            out=gt_l[sl][:].rearrange("p a b -> p (a b)"), in_=g_scr[ci, :, :]),
            deps=[gla["k_gate"][sl]], dsem=f"glg{sl}")
        vsl = ci % 3
        lv = s.add("sp", lambda g, ci=ci, vsl=vsl: g.dma_start(
            out=vt_l[vsl][:], in_=v_scr[ci, :, :]), deps=[gla["v_gate"][vsl]],
            dsem=f"glv{vsl}")
        lq = None
        if main:
            lq = s.add("sp", lambda g, ci=ci, sl=sl: g.dma_start(
                out=qt_l[sl][:].rearrange("p a b -> p (a b)"),
                in_=q_scr[ci - NP, :, :]), deps=[gla["k_gate"][sl]], dsem=f"glq{sl}")
        sc_last = None
        for h in range(NH):
            sc_last = s.add("dve", lambda v, h=h, sl=sl: v.tensor_tensor_scan(
                cum_b[:, h, :], gt_l[sl][:, h, :], ones_f[:, 0:CH], 0.0,
                ALU.add, ALU.mult),
                deps=[lg, init_c] + ([gla["act_last"]] if h == 0 else []))
        cumc = s.add("dve", lambda v, sl=sl: v.tensor_scalar(
            tiny[:, sl, :, 0:1], cum_b[:, :, CH - 1:CH], 1.0 / 16.0, None,
            op0=ALU.mult), deps=[sc_last, gla["tiny_gate"][sl]])
        lc = s.add("act", lambda a, sl=sl: a.activation(
            tiny[:, sl, :, 1:2], tiny[:, sl, :, 0:1], AF.Exp), deps=[cumc])
        kd = s.add("act", lambda a, sl=sl: a.activation(
            kd_all[sl][:].rearrange("p a b -> p (a b)"),
            cum_b[:].rearrange("p a b -> p (a b)"), AF.Exp, scale=cst[:, 3:4]),
            deps=[sc_last, init_c])
        kt_op = s.add("dve", lambda v, sl=sl: v.tensor_mul(
            ktq_all[sl][:].rearrange("p a b -> p (a b)"),
            kt_l[sl][:].rearrange("p a b -> p (a b)"),
            kd_all[sl][:].rearrange("p a b -> p (a b)")),
            deps=[kd, lk, gla["pe_gate"][sl]])
        ke_op = s.add("dve", lambda v, sl=sl: v.tensor_mul(
            ke_all[sl][:], ktq_all[sl][:],
            tiny[:, sl, :, 1:2].to_broadcast([128, NH, CH])),
            deps=[kt_op, lc, gla["pe_gate"][sl]])
        qt_op = None
        lqe = None
        if main:
            lqe = s.add("act", lambda a, sl=sl: a.activation(
                lq_all[sl][:].rearrange("p a b -> p (a b)"),
                cum_b[:].rearrange("p a b -> p (a b)"), AF.Exp,
                scale=cst[:, 2:3], bias=cst[:, 4:5]), deps=[sc_last])
            qt_op = s.add("dve", lambda v, sl=sl: v.tensor_mul(
                qt_all[sl][:].rearrange("p a b -> p (a b)"),
                qt_l[sl][:].rearrange("p a b -> p (a b)"),
                lq_all[sl][:].rearrange("p a b -> p (a b)")),
                deps=[lqe, lq, gla["pe_gate"][sl]])
        gla["k_gate"][sl] = ke_op if not main else qt_op
        gla["act_last"] = lqe if main else kd
        return dict(sl=sl, vsl=vsl, lv=lv, lc=lc, kt_op=kt_op, ke_op=ke_op, qt_op=qt_op)

    def gla_stage2(st, main, S_cur, S_nxt):
        sl = st["sl"]
        vsl = st["vsl"]
        lv = st["lv"]
        sbf = None
        if main:
            sbf = s.add("act", lambda a: a.copy(S_bf[:], S_cur[:]),
                        deps=[gla["state_dep"], gla["sbf_gate"]])
        t1 = s.add("dve", lambda v, sl=sl: v.tensor_mul(
            S_nxt.rearrange("p (a b) -> p a b", a=NH),
            S_cur.rearrange("p (a b) -> p a b", a=NH),
            tiny[:, sl, :, 1:2].to_broadcast([128, NH, CH])),
            deps=[st["lc"], gla["state_dep"]] + ([sbf] if sbf else []))
        last_dve = t1
        o_ev = None
        o_grp = None
        p_last = None
        for g4 in range(4):
            gsl = g4 % 2
            bkA, bkO, bkP = gsl, 2 + gsl, 4 + gsl
            hs = range(g4 * 4, g4 * 4 + 4)
            if main:
                def atg(pe, hs=hs, bkA=bkA, sl=sl):
                    r = None
                    for j, h in enumerate(hs):
                        r = pe.matmul(bank[bkA][:, j * 128:(j + 1) * 128],
                                      ktq_all[sl][:, h, :], qt_all[sl][:, h, :],
                                      start=True, stop=True)
                    return r
                at_grp = s.add("pe", atg,
                               deps=[st["kt_op"], st["qt_op"], bank_rd[bkA]])
                atm = s.add("dve", lambda v, gsl=gsl, bkA=bkA: v.tensor_mul(
                    atm4[:, gsl, :, :], bank[bkA][:], mask_bf[:]),
                    deps=[at_grp, l_mask, gla["atm_gate"][gsl]])
                bank_rd[bkA] = atm
                def og(pe, hs=hs, gsl=gsl, bkO=bkO, sl=sl, vsl=vsl):
                    r = None
                    for j, h in enumerate(hs):
                        pe.matmul(bank[bkO][:, j * 128:(j + 1) * 128],
                                  vt_l[vsl][:, h * 128:(h + 1) * 128],
                                  atm4[:, gsl, j, :], start=True, stop=False)
                        r = pe.matmul(bank[bkO][:, j * 128:(j + 1) * 128],
                                      S_bf[:, h * DK:(h + 1) * DK],
                                      qt_all[sl][:, h, :], start=False, stop=True)
                    return r
                o_grp = s.add("pe", og, deps=[atm, lv, sbf, bank_rd[bkO]])
                gla["atm_gate"][gsl] = o_grp
                o_ev = s.add("act", lambda a, g4=g4, bkO=bkO, sl=sl: a.copy(
                    o_fm[sl][:, g4 * 4:(g4 + 1) * 4, :], bank[bkO][:]),
                    deps=[o_grp, gla["ofm_pe_gate"]])
                bank_rd[bkO] = o_ev
                gla["o_ev_last"] = o_ev
            tp_area = bank_t[:, 0:512] if gsl == 0 else bank6_bf[:, 0:512]
            def tpg(pe, hs=hs, sl=sl, tp_area=tp_area):
                r = None
                for j, h in enumerate(hs):
                    r = pe.transpose(
                        tp_area[:, j * 128:(j + 1) * 128],
                        ke_all[sl][:, h, :], ident_bf[:])
                return r
            tp_grp = s.add("pe", tpg,
                           deps=[st["ke_op"], gla["tp_gate"][gsl], l_id])
            ket_ev = s.add("act", lambda a, gsl=gsl, tp_area=tp_area: a.copy(
                ket4[:, gsl, :, :], tp_area[:]),
                deps=[tp_grp, gla["ket_gate"][gsl]])
            gla["tp_gate"][gsl] = ket_ev
            def pg(pe, hs=hs, gsl=gsl, bkP=bkP, vsl=vsl):
                r = None
                for j, h in enumerate(hs):
                    r = pe.matmul(bank[bkP][:, j * 128:(j + 1) * 128],
                                  ket4[:, gsl, j, :],
                                  vt_l[vsl][:, h * 128:(h + 1) * 128],
                                  start=True, stop=True)
                return r
            p_grp = s.add("pe", pg, deps=[ket_ev, lv, bank_rd[bkP]])
            gla["ket_gate"][gsl] = p_grp
            p_add = s.add("dve", lambda v, g4=g4, bkP=bkP: v.tensor_add(
                S_nxt[:, g4 * 512:(g4 + 1) * 512],
                S_nxt[:, g4 * 512:(g4 + 1) * 512], bank[bkP][:]),
                deps=[p_grp, t1])
            bank_rd[bkP] = p_add
            last_dve = p_add
            p_last = p_grp
        gla["v_gate"][vsl] = p_last
        gla["pe_gate"][sl] = p_last
        gla["tiny_gate"][sl] = last_dve
        gla["state_dep"] = last_dve
        if main:
            gla["sbf_gate"] = o_grp
        return last_dve

    bar_g = s.barrier()
    lwo = None
    for dch in range(NDCH):
        lwo = s.add("pool", lambda g, dch=dch: g.dma_start(
            out=wo_sb[:, dch, :], in_=wo_d[dch * 128:(dch + 1) * 128, :]),
            deps=[bar_g] if dch == 0 else [], dsem="wo")

    gla["k_gate"] = [bar_g, bar_g]
    gla["v_gate"] = [bar_g, bar_g, bar_g]
    gla["tp_gate"] = [bar_g, bar_g]
    gla["pe_gate"] = [bar_g, bar_g]
    gla["atm_gate"] = [bar_g, bar_g]
    gla["ket_gate"] = [bar_g, bar_g]
    for _bk in range(7):
        bank_rd[_bk] = bar_g

    gla["tiny_gate"] = [bar_g, bar_g]
    st_pipe = gla_stage1(0, False)
    for c in range(NP):
        st_cur = st_pipe
        if c + 1 < NP:
            st_pipe = gla_stage1(c + 1, False)
        S_cur = S_a if c % 2 == 0 else S_b
        S_nxt = S_b if c % 2 == 0 else S_a
        gla_stage2(st_cur, False, S_cur, S_nxt)
    S_fin = S_a if NP % 2 == 0 else S_b
    S_oth = S_b if NP % 2 == 0 else S_a
    f_op = s.add("dve", lambda v: v.tensor_scalar(
        S_fin[:], S_fin[:], cst[:, 5:6], None, op0=ALU.mult),
        deps=[gla["state_dep"], l_flag])
    gla["state_dep"] = f_op

    if stop_after == "p1":
        return _stop_emit()
    wo_banks = [0, 1, 2, 3]
    attn_done = []
    attn_gate = [None, None]
    wo_pe_gate = [None, None]
    st_pipe = gla_stage1(NP, True)
    for c in range(NM):
        ci = NP + c
        st_cur = st_pipe
        if c + 1 < NM:
            st_pipe = gla_stage1(ci + 1, True)
        S_cur = S_fin if c % 2 == 0 else S_oth
        S_nxt = S_oth if c % 2 == 0 else S_fin
        gla["ofm_pe_gate"] = wo_pe_gate[c % 2]
        gla_stage2(st_cur, True, S_cur, S_nxt)
        asl = 0
        ev_deps = []
        wo_mm_last = None
        for dmg in range(4):
            bk = wo_banks[dmg]
            csl = ci % 2
            def wmm(pe, dmg=dmg, bk=bk, csl=csl):
                r = None
                for fcc in range(NDCH):
                    r = pe.matmul(bank[bk][:], o_fm[csl][:, fcc, :],
                                  wo_sb[:, fcc, dmg * 512:(dmg + 1) * 512],
                                  start=(fcc == 0), stop=(fcc == NDCH - 1))
                return r
            mm = s.add("pe", wmm, deps=[gla["o_ev_last"], lwo, bank_rd[bk]])
            e1 = s.add("dve", lambda v, dmg=dmg, bk=bk, asl=asl: v.tensor_copy(
                attn_t[asl][:, dmg * 512:(dmg + 1) * 512], bank[bk][:]),
                deps=[mm, attn_gate[asl]])
            bank_rd[bk] = e1
            ev_deps.append(e1)
            wo_mm_last = mm
        wo_pe_gate[c % 2] = wo_mm_last
        wr = s.add("pool", lambda g, c=c, asl=asl: g.dma_start(
            out=at_scr[c, :, :], in_=attn_t[asl][:]), deps=ev_deps, dsem=f"nat{asl}")
        attn_gate[asl] = wr
        attn_done.append(wr)

    if stop_after == "p3":
        return _stop_emit()
    bar_m1 = s.barrier()
    for _bk in range(7):
        bank_rd[_bk] = bar_m1
    xng_rd_gate2 = [bar_m1, bar_m1]
    xng_pe2 = [bar_m1, bar_m1]
    tp_gate2 = bar_m1
    w_all_done = None
    for tc in range(NM):
        xsl = tc % 2
        gidx = (TP + tc * 128) // PW
        tci = tc % 4
        lx = s.add("sp", lambda g, gidx=gidx, tci=tci, xsl=xsl: g.dma_start(
            out=xng_m[xsl][:, :, 0:128],
            in_=xn_scr[gidx, :, :].rearrange("p (c t) -> p c t", t=PW)[
                :, :, tci * 128:(tci + 1) * 128]),
            deps=[xng_rd_gate2[xsl], xng_pe2[xsl]], dsem=f"apx{xsl}")
        xng_rd_gate2[xsl] = lx
        bk = 6
        def rmm(pe, xsl=xsl, bk=bk):
            r = None
            for dch in range(NDCH):
                r = pe.matmul(bank[bk][:, 0:NE], xng_m[xsl][:, dch, 0:128],
                              rw_sb[:, dch, :], start=(dch == 0),
                              stop=(dch == NDCH - 1))
            return r
        mm = s.add("pe", rmm, deps=[lx, l_rw, bank_rd[bk]])
        xng_pe2[xsl] = mm
        e1 = s.add("dve", lambda v, bk=bk: v.tensor_copy(logit_t[:], bank[bk][:, 0:NE]),
                   deps=[mm])
        bank_rd[bk] = e1
        m8 = s.add("dve", lambda v: v.max(out=top8[:], in_=logit_t[:]))
        nb = s.add("dve", lambda v: v.tensor_scalar(
            rtiny[:, 0:1], top8[:, 0:1], -1.0, None, op0=ALU.mult))
        ex = s.add("act", lambda a: a.activation(
            wexp[:], logit_t[:], AF.Exp, bias=rtiny[:, 0:1]), deps=[nb])
        msk = s.add("dve", lambda v: v.tensor_scalar(
            rmask[:], logit_t[:], top8[:, 3:4], None, op0=ALU.is_ge), deps=[ex])
        wmul = s.add("dve", lambda v: v.tensor_mul(wexp[:], wexp[:], rmask[:]),
                     deps=[ex])
        wsum = s.add("dve", lambda v: v.reduce_sum(rtiny[:, 1:2], wexp[:], axis=AX.X))
        wrec = s.add("dve", lambda v: v.reciprocal(rtiny[:, 0:1], rtiny[:, 1:2]),
                     deps=[ex])
        wnrm = s.add("dve", lambda v: v.tensor_scalar(
            wexp_bf[:], wexp[:], rtiny[:, 0:1], None, op0=ALU.mult))
        tp = s.add("pe", lambda pe: pe.transpose(
            bank_t[0:NE, 512:640], wexp_bf[:], ident_bf[:]),
            deps=[wnrm, tp_gate2, l_id])
        ev1 = s.add("act", lambda a, tc=tc: a.copy(
            w_all[0:NE, tc * 128:(tc + 1) * 128], bank_t[0:NE, 512:640]),
            deps=[tp])
        tp_gate2 = ev1
        w_all_done = ev1

    if stop_after == "router":
        return _stop_emit()
    guw_pe_gate = [bar_m1] * 4
    wbc_dve_gate = [None, None]
    wbc_ev = [[], []]
    sg_dve_gate = [None] * NIC
    hb_wr_gate = [None] * NIC
    m1_fin = None

    def emit_wbc_mm(e, ng):
        slot = e % 2
        bk = 6
        mm = s.add("pe", lambda pe, e=e, ng=ng, bk=bk: pe.matmul(
            bank[bk][:], sel_sb[:, e, :], w_all[:, ng * PW:(ng + 1) * PW],
            start=True, stop=True),
            deps=[w_all_done, l_sel, bank_rd[bk], wbc_dve_gate[slot]])
        e1 = s.add("act", lambda a, ng=ng, bk=bk, slot=slot: a.copy(
            w_bc[slot][:, ng * PW:(ng + 1) * PW], bank[bk][:]),
            deps=[mm])
        bank_rd[bk] = e1
        wbc_ev[slot].append(e1)

    for ng in range(TM // PW):
        emit_wbc_mm(0, ng)

    for e in range(NE):
        gslot, uslot = e % 2, 2 + e % 2
        lgw = None
        for dch in range(NDCH):
            lgw = s.add("pool", lambda g, e=e, dch=dch, gslot=gslot: g.dma_start(
                out=guw[gslot][:, dch, :], in_=gwt_d[e, dch * 128:(dch + 1) * 128, :]),
                deps=[guw_pe_gate[gslot]] if dch == 0 else [], dsem=f"gw{gslot}")
        luw = None
        for dch in range(NDCH):
            luw = s.add("pool", lambda g, e=e, dch=dch, uslot=uslot: g.dma_start(
                out=guw[uslot][:, dch, :], in_=uwt_d[e, dch * 128:(dch + 1) * 128, :]),
                deps=[guw_pe_gate[uslot]] if dch == 0 else [], dsem=f"uw{uslot - 2}")
        bc_mine = list(wbc_ev[e % 2])
        wbc_ev[(e + 1) % 2] = []
        nxt = [(e + 1, ng) for ng in range(TM // PW)] if e + 1 < NE else []
        gate_sil = {}
        last_pe_of_e = None

        def m1_load_xn(tg):
            xsl = tg % 2
            h = s.add("sp", lambda g, tg=tg, xsl=xsl: g.dma_start(
                out=xng_m[xsl][:].rearrange("p a b -> p (a b)"),
                in_=xn_scr[TP // PW + tg, :, :]),
                deps=[xng_rd_gate2[xsl], xng_pe2[xsl]], dsem=f"apx{xsl}")
            xng_rd_gate2[xsl] = h
            return h

        if e == 0:
            m1_lx = {0: m1_load_xn(0)}
        for tg in range(TM // PW):
            xsl = tg % 2
            lx = m1_lx.pop(tg) if tg in m1_lx else m1_load_xn(tg)
            for ic in range(NIC):
                bk = ic % 3
                def gmm(pe, ic=ic, xsl=xsl, bk=bk, gslot=gslot):
                    r = None
                    for dch in range(NDCH):
                        r = pe.matmul(bank[bk][:],
                                      guw[gslot][:, dch, ic * 128:(ic + 1) * 128],
                                      xng_m[xsl][:, dch, :],
                                      start=(dch == 0), stop=(dch == NDCH - 1))
                    return r
                mm = s.add("pe", gmm, deps=[lgw, lx, bank_rd[bk]])
                if nxt:
                    emit_wbc_mm(*nxt.pop(0))
                sil = s.add("act", lambda a, ic=ic, bk=bk: a.activation(
                    sg_b[:, ic, :], bank[bk][:], AF.Silu),
                    deps=[mm, sg_dve_gate[ic]])
                bank_rd[bk] = sil
                gate_sil[ic] = sil
            nxt_tg = tg + 1
            if nxt_tg < TM // PW and nxt_tg not in m1_lx:
                m1_lx[nxt_tg] = m1_load_xn(nxt_tg)
            for ic in range(NIC):
                bk = 3 + ic % 3
                def umm(pe, ic=ic, xsl=xsl, bk=bk, uslot=uslot):
                    r = None
                    for dch in range(NDCH):
                        r = pe.matmul(bank[bk][:],
                                      guw[uslot][:, dch, ic * 128:(ic + 1) * 128],
                                      xng_m[xsl][:, dch, :],
                                      start=(dch == 0), stop=(dch == NDCH - 1))
                    return r
                mm = s.add("pe", umm, deps=[luw, lx, bank_rd[bk]])
                if nxt:
                    emit_wbc_mm(*nxt.pop(0))
                h1 = s.add("dve", lambda v, ic=ic, bk=bk: v.tensor_mul(
                    h_b[:, ic, :], sg_b[:, ic, :], bank[bk][:]),
                    deps=[mm, gate_sil[ic], hb_wr_gate[ic]])
                h2 = s.add("dve", lambda v, ic=ic, tg=tg, e=e: v.tensor_mul(
                    h_b[:, ic, :], h_b[:, ic, :],
                    w_bc[e % 2][:, tg * PW:(tg + 1) * PW]),
                    deps=bc_mine)
                sg_dve_gate[ic] = h1
                wbc_dve_gate[e % 2] = h2
                bank_rd[bk] = h1
                wl = None
                for j in range(4):
                    tt = tg * 4 + j
                    def hw(g, e=e, tt=tt, ic=ic, j=j):
                        return g.dma_start(
                            out=h_scr[e, tt, ic * 128:(ic + 1) * 128, :],
                            in_=h_b[:, ic, j * 128:(j + 1) * 128])
                    wl = s.add("sp", hw, deps=[h2], dsem=f"hw{ic}")
                hb_wr_gate[ic] = wl
                m1_fin = wl
                last_pe_of_e = mm
            xng_pe2[xsl] = last_pe_of_e
            if tg == TM // PW - 1 and e + 1 < NE:
                m1_lx = {0: m1_load_xn(0)}
        guw_pe_gate[gslot] = last_pe_of_e
        guw_pe_gate[uslot] = last_pe_of_e

    if stop_after == "m1":
        return _stop_emit()
    m1_join = s.add("pool", lambda g: g.memset(dummy[:], 0.0), deps=hb_wr_gate)
    attn_join = s.add("pool", lambda g: g.memset(dummy[:], 0.0),
                      deps=[attn_done[-1], attn_done[-2]])
    bar_m2 = s.barrier()

    NMH = NM // 2
    NG = NE // EG
    out_done = []
    dw_gate = [bar_m2] * (2 * EG)
    hin_gate = [bar_m2] * (4 * EG)
    for bk in range(7):
        bank_rd[bk] = bar_m2
    half_gate = bar_m2
    for hh in range(2):
        a1s = []
        for tl in range(NMH):
            tt = hh * NMH + tl
            a1s.append(s.add("act", lambda a, tt=tt, tl=tl: a.dma_start(
                out=acc[:, tl, :], in_=xm[tt * 128:(tt + 1) * 128, :]),
                deps=[half_gate], dsem=f"mxi{tl}"))
        acc_init = []
        for tl in range(NMH):
            tt = hh * NMH + tl
            acc_init.append(s.add("pool", lambda g, tt=tt, tl=tl: g.dma_start(
                out=acc[:, tl, :], in_=at_scr[tt, :, :], accum_op=ALU.add),
                deps=[a1s[tl], attn_join, m1_join], dsem=f"mai{tl}"))
        acc_last = [None] * NMH
        for grp in range(NG):
            dws = []
            for ei in range(EG):
                e = grp * EG + ei
                slot = (grp % 2) * EG + ei
                last = None
                for ic in range(NIC):
                    last = s.add("pool", lambda g, e=e, ic=ic, slot=slot: g.dma_start(
                        out=dwb[slot][:, ic, :],
                        in_=dwt_d[e, ic * 128:(ic + 1) * 128, :]),
                        deps=[dw_gate[slot]] if ic == 0 else [], dsem=f"dw{slot}")
                dws.append(last)
            for tl in range(NMH):
                tt = hh * NMH + tl
                hins = []
                hslots = []
                for ei in range(EG):
                    e = grp * EG + ei
                    slot = (tl % 4) * EG + ei
                    hop = s.add("sp", lambda g, e=e, tt=tt, slot=slot: g.dma_start(
                        out=h_in[slot][:],
                        in_=h_scr[e, tt, :, :].rearrange("(c p) t -> p c t", p=128)),
                        deps=[hin_gate[slot], m1_join if grp == 0 else None],
                        dsem=f"hi{slot}")
                    hins.append(hop)
                    hslots.append(slot)
                last_mm = None
                for dmg in range(4):
                    bk = (tl * 4 + dmg) % 6
                    def dmm(pe, tl=tl, dmg=dmg, bk=bk, grp=grp):
                        r = None
                        first = True
                        for ei in range(EG):
                            slot = (tl % 4) * EG + ei
                            for ic in range(NIC):
                                r = pe.matmul(
                                    bank[bk][:], h_in[slot][:, ic, :],
                                    dwb[(grp % 2) * EG + ei][:, ic,
                                                             dmg * 512:(dmg + 1) * 512],
                                    start=first, stop=(ei == EG - 1 and ic == NIC - 1))
                                first = False
                        return r
                    mm = s.add("pe", dmm, deps=hins + dws + [bank_rd[bk]])
                    ad = s.add("dve", lambda v, tl=tl, dmg=dmg, bk=bk: v.tensor_add(
                        acc[:, tl, dmg * 512:(dmg + 1) * 512],
                        acc[:, tl, dmg * 512:(dmg + 1) * 512], bank[bk][:]),
                        deps=[mm, acc_last[tl], acc_init[tl]])
                    bank_rd[bk] = ad
                    acc_last[tl] = ad
                    last_mm = mm
                for sl2 in hslots:
                    hin_gate[sl2] = last_mm
                for ei in range(EG):
                    dw_gate[(grp % 2) * EG + ei] = last_mm
        half_outs = []
        for tl in range(NMH):
            tt = hh * NMH + tl
            wop = s.add("sp", lambda g, tt=tt, tl=tl: g.dma_start(
                out=out_d[tt * 128:(tt + 1) * 128, :], in_=acc[:, tl, :]),
                deps=[acc_last[tl]], dsem="outw")
            half_outs.append(wop)
            out_done.append(wop)
        half_gate = half_outs[-1]
    s.add("sp", lambda g: g.nop(), deps=out_done)

    built = s.build()
    for cm in reversed(built):
        cm.__exit__(None, None, None)
    for cm in reversed(ctxs):
        cm.__exit__(None, None, None)
    return nc, s


def host_inputs(x, Wq, Wk, Wv, Wg, Wo, router_w, gate_w, up_w, down_w,
                NP=16, NM=16):
    x = np.ascontiguousarray(np.asarray(x, np.float32))
    B, S, _ = x.shape
    NE = np.asarray(gate_w).shape[0]
    rwt = np.ascontiguousarray(np.asarray(router_w, np.float32).T)
    gwt = np.ascontiguousarray(np.asarray(gate_w, np.float32).transpose(0, 2, 1))
    uwt = np.ascontiguousarray(np.asarray(up_w, np.float32).transpose(0, 2, 1))
    dwt = np.ascontiguousarray(np.asarray(down_w, np.float32).transpose(0, 2, 1))
    ident = np.eye(128, dtype=np.float32)
    maskst = np.ascontiguousarray(np.tile(np.triu(np.ones((128, 128), np.float32)), (1, 4)))
    sel = np.zeros((NE, NE, 128), np.float32)
    for e in range(NE):
        sel[e, e, :] = 1.0
    sel = np.ascontiguousarray(sel.reshape(NE, NE * 128))
    TP, TM = NP * CH, NM * CH
    n_half = S // TM
    in_maps = []
    n_cores = B * n_half
    for c in range(n_cores):
        b, m = c // n_half, c % n_half
        in_maps.append(dict(
            xp=np.ascontiguousarray(x[b, 0:TP]),
            xm=np.ascontiguousarray(x[b, TM * m:TM * (m + 1)]),
            wq=np.ascontiguousarray(np.asarray(Wq, np.float32)),
            wk=np.ascontiguousarray(np.asarray(Wk, np.float32)),
            wv=np.ascontiguousarray(np.asarray(Wv, np.float32)),
            wg=np.ascontiguousarray(np.asarray(Wg, np.float32)),
            wo=np.ascontiguousarray(np.asarray(Wo, np.float32)),
            rwt=rwt, gwt=gwt, uwt=uwt, dwt=dwt,
            flag=np.full((128, 1), float(min(m, 1)), np.float32),
            ident=ident, maskst=maskst, sel=sel,
        ))
    return in_maps


def kernel(L0_attn_x, L0_attn_Wq, L0_attn_Wk, L0_attn_Wv, L0_attn_Wg, L0_attn_Wo,
           L0_ffn_router_w, L0_ffn_gate_w, L0_ffn_up_w, L0_ffn_down_w):
    x = np.ascontiguousarray(np.asarray(L0_attn_x, np.float32))
    B, S, _ = x.shape
    NE = np.asarray(L0_ffn_gate_w).shape[0]
    nc, _ = build_program(NP=16, NM=16, NE=NE, EG=2)
    in_maps = host_inputs(x, L0_attn_Wq, L0_attn_Wk, L0_attn_Wv, L0_attn_Wg,
                          L0_attn_Wo, L0_ffn_router_w, L0_ffn_gate_w,
                          L0_ffn_up_w, L0_ffn_down_w)
    res = run_bass_kernel_spmd(nc, in_maps, core_ids=list(range(8)))
    outp = np.empty((B, S, D), np.float32)
    for c in range(8):
        b, m = c // 2, c % 2
        outp[b, 2048 * m:2048 * (m + 1)] = res.results[c]["out"]
    return outp


# version 13
# speedup vs baseline: 1.1672x; 1.1672x over previous
import math
import numpy as np
import ml_dtypes
import concourse.bass as bass
import concourse.mybir as mybir
from concourse.bass_utils import run_bass_kernel_spmd

F32 = mybir.dt.float32
BF16 = mybir.dt.bfloat16
AF = mybir.ActivationFunctionType
ALU = mybir.AluOpType
AX = mybir.AxisListType

D = 2048
NH = 16
DK = 128
CH = 128
IF = 768
NIC = IF // 128
NDCH = D // 128
PW = 512

ENGINES = ("pe", "act", "dve", "pool", "sp")


class OpHandle:
    __slots__ = ("key", "cum")
    def __init__(self, key, cum):
        self.key = key
        self.cum = cum


class Sched:

    def __init__(self, nc):
        self.nc = nc
        self.items = {e: [] for e in ENGINES}
        self.cum = {}
        self.last_compute = {}
        self.last_on_key = {}
        self.n_waits = 0
        self.n_ops = 0

    def add(self, engine, fn, deps=(), dsem=None, inc=None, chain=True):
        if dsem is None:
            key = ("c", engine)
            inc = 1 if inc is None else inc
        else:
            key = ("d", dsem)
            inc = 16 if inc is None else inc
        deps = [d for d in deps if d is not None]
        if dsem is None and engine in ("act", "dve", "pool") and chain:
            prev = self.last_compute.get(engine)
            if prev is not None:
                deps.append(prev)
        self.cum[key] = self.cum.get(key, 0) + inc
        h = OpHandle(key, self.cum[key])
        if dsem is None and engine in ("act", "dve", "pool"):
            self.last_compute[engine] = h
        self.last_on_key[key] = h
        self.items[engine].append((fn, tuple(deps), inc, key))
        self.n_ops += 1
        return h

    def barrier(self):
        deps = list(self.last_on_key.values())
        return self.add("pool", lambda g: g.memset(self.nc_dummy[:], 0.0), deps=deps)

    def build(self):
        nc = self.nc
        sems = {}
        ctxs = []
        for key in self.cum:
            cm = nc.semaphore(f"S_{key[0]}_{key[1]}")
            sems[key] = cm.__enter__()
            ctxs.append(cm)
        blk_cm = nc.Block()
        block = blk_cm.__enter__()
        ctxs.append(blk_cm)

        def make_body(e):
            eng = {"pe": nc.tensor, "act": nc.scalar, "dve": nc.vector,
                   "pool": nc.gpsimd, "sp": nc.sync}[e]
            items = self.items[e]
            sched = self

            def body(_):
                waited = {}
                for fn, deps, inc, key in items:
                    need = {}
                    for dd in deps:
                        if dd.key == key == ("c", "pe"):
                            continue
                        if dd.cum > need.get(dd.key, 0):
                            need[dd.key] = dd.cum
                    for k, v in need.items():
                        if v > waited.get(k, 0):
                            eng.wait_ge(sems[k], v)
                            waited[k] = v
                            sched.n_waits += 1
                    instr = fn(eng)
                    instr.then_inc(sems[key], inc)
            return body

        for e in ENGINES:
            if not self.items[e]:
                continue
            dec = {"pe": block.tensor, "act": block.scalar, "dve": block.vector,
                   "pool": block.gpsimd, "sp": block.sync}[e]
            dec(make_body(e))
        return ctxs


def build_program(NP=16, NM=16, NE=32, EG=2, stop_after=None):
    TP, TM = NP * CH, NM * CH
    NTG = (TP + TM) // PW
    NTGM = TM // PW
    NCA = NP + NM

    nc = bass.Bass("TRN2", target_bir_lowering=False)

    xp = nc.dram_tensor("xp", [TP, D], F32, kind="ExternalInput").ap()
    xm = nc.dram_tensor("xm", [TM, D], F32, kind="ExternalInput").ap()
    wq_d = nc.dram_tensor("wq", [D, D], F32, kind="ExternalInput").ap()
    wk_d = nc.dram_tensor("wk", [D, D], F32, kind="ExternalInput").ap()
    wv_d = nc.dram_tensor("wv", [D, D], F32, kind="ExternalInput").ap()
    wg_d = nc.dram_tensor("wg", [D, D], F32, kind="ExternalInput").ap()
    wo_d = nc.dram_tensor("wo", [D, D], F32, kind="ExternalInput").ap()
    rwt_d = nc.dram_tensor("rwt", [D, NE], F32, kind="ExternalInput").ap()
    gwt_d = nc.dram_tensor("gwt", [NE, D, IF], F32, kind="ExternalInput").ap()
    uwt_d = nc.dram_tensor("uwt", [NE, D, IF], F32, kind="ExternalInput").ap()
    dwt_d = nc.dram_tensor("dwt", [NE, IF, D], F32, kind="ExternalInput").ap()
    flag_d = nc.dram_tensor("flag", [128, 1], F32, kind="ExternalInput").ap()
    ident_d = nc.dram_tensor("ident", [128, 128], F32, kind="ExternalInput").ap()
    mask_d = nc.dram_tensor("maskst", [128, 512], F32, kind="ExternalInput").ap()
    sel_d = nc.dram_tensor("sel", [NE, NE * 128], F32, kind="ExternalInput").ap()
    out_d = nc.dram_tensor("out", [TM, D], F32, kind="ExternalOutput").ap()

    xn_scr = nc.dram_tensor("xn_scr", [NTG, 128, NDCH * PW], BF16, kind="Internal").ap()
    k_scr = nc.dram_tensor("k_scr", [NCA, 128, NH * CH], BF16, kind="Internal").ap()
    g_scr = nc.dram_tensor("g_scr", [NCA, 128, NH * CH], BF16, kind="Internal").ap()
    q_scr = nc.dram_tensor("q_scr", [NM, 128, NH * CH], BF16, kind="Internal").ap()
    v_scr = nc.dram_tensor("v_scr", [NCA, CH, D], BF16, kind="Internal").ap()
    h_scr = nc.dram_tensor("h_scr", [NE, NM, IF, CH], BF16, kind="Internal").ap()
    at_scr = nc.dram_tensor("at_scr", [NM, CH, D], F32, kind="Internal").ap()

    s = Sched(nc)
    ctxs = []

    def sb(name, shape, dtype):
        cm = nc.sbuf_tensor(name, shape, dtype)
        t = cm.__enter__()
        ctxs.append(cm)
        return t

    def ps(name, shape, dtype=F32):
        cm = nc.psum_tensor(name, shape, dtype)
        t = cm.__enter__()
        ctxs.append(cm)
        return t

    bank = [ps(f"bank{i}", [128, PW], F32) for i in range(7)]
    bank_t = ps("bank_t", [128, 1024], BF16)
    bank6_bf = bank[6].bitcast(BF16)

    dummy = sb("dummy_sb", [1, 2], F32)
    s.nc_dummy = dummy
    cst = sb("cst", [128, 8], F32)
    ident_bf = sb("ident_bf", [128, 128], BF16)
    mask_bf = sb("mask_bf", [128, 512], BF16)
    ones_f = sb("ones_f", [128, 128], F32)
    S_a = sb("S_a", [128, NH * DK], F32)
    S_b = sb("S_b", [128, NH * DK], F32)
    S_bf = sb("S_bf", [128, NH * DK], BF16)
    lns = sb("lns", [128, 4, 8], F32)
    tiny = sb("tiny", [128, 2, NH, 2], F32)
    rw_sb = sb("rw_sb", [128, NDCH, NE], BF16)
    w_all = sb("w_all", [NE, TM], BF16)
    sel_sb = sb("sel_sb", [NE, NE, 128], BF16)
    logit_t = sb("logit_t", [128, NE], F32)
    rmask = sb("rmask", [128, NE], F32)
    top8 = sb("top8", [128, 8], F32)
    wexp = sb("wexp", [128, NE], F32)
    wexp_bf = sb("wexp_bf", [128, NE], BF16)
    rtiny = sb("rtiny", [128, 2], F32)

    OVB = 176256
    arena32 = sb("arena", [128, OVB // 4], F32)
    arena16 = arena32.bitcast(BF16)

    def vw(off, shape, dtype):
        assert off % 64 == 0
        n = 1
        for d in shape[1:]:
            n *= d
        if dtype == F32:
            ap = arena32.ap()[0:shape[0], off // 4: off // 4 + n]
        else:
            ap = arena16.ap()[0:shape[0], off // 2: off // 2 + n]
        if len(shape) == 3:
            ap = ap.rearrange("p (a b) -> p a b", a=shape[1])
        elif len(shape) == 4:
            ap = ap.rearrange("p (a b c) -> p a b c", a=shape[1], b=shape[2])
        assert off + n * (4 if dtype == F32 else 2) <= OVB
        return ap


    def _stop_emit():
        outs = []
        for tt in range(NM):
            outs.append(s.add("sp", lambda g, tt=tt: g.dma_start(
                out=out_d[tt * 128:(tt + 1) * 128, :],
                in_=arena32.ap()[:, 0:D]), deps=[s.barrier()], dsem="outw"))
        s.add("sp", lambda g: g.nop(), deps=outs)
        built = s.build()
        for cm in reversed(built):
            cm.__exit__(None, None, None)
        for cm in reversed(ctxs):
            cm.__exit__(None, None, None)
        return nc, s
    xf = [vw(i * 8192, [128, D], F32) for i in range(4)]
    xn_tm = [vw(32768 + i * 4096, [128, D], BF16) for i in range(4)]
    xng = [vw(49152 + i * 16384, [128, NDCH, PW], BF16) for i in range(4)]
    wbuf = [vw(114688, [128, NDCH, 512], BF16), vw(131072, [128, NDCH, 512], BF16)]
    ev = [vw(147456 + i * 1024, [128, PW], BF16) for i in range(4)]
    gzmin = vw(151552, [128, PW], F32)
    gtmp = vw(153600, [128, PW], F32)
    sqd = vw(155648, [128, D], BF16)

    kt_l = [vw(0, [128, NH, CH], BF16), vw(4096, [128, NH, CH], BF16)]
    gt_l = [vw(8192, [128, NH, CH], BF16), vw(12288, [128, NH, CH], BF16)]
    qt_l = [vw(16384, [128, NH, CH], BF16), vw(20480, [128, NH, CH], BF16)]
    vt_l = [vw(24576, [128, D], BF16), vw(28672, [128, D], BF16),
            vw(167936, [128, D], BF16)]
    cum_b = vw(32768, [128, NH, CH], F32)
    lq_all = [vw(40960, [128, NH, CH], BF16), vw(45056, [128, NH, CH], BF16)]
    kd_all = [vw(49152, [128, NH, CH], BF16), vw(53248, [128, NH, CH], BF16)]
    qt_all = [vw(57344, [128, NH, CH], BF16), vw(61440, [128, NH, CH], BF16)]
    ktq_all = [vw(65536, [128, NH, CH], BF16), vw(69632, [128, NH, CH], BF16)]
    ke_all = [vw(73728, [128, NH, CH], BF16), vw(77824, [128, NH, CH], BF16)]
    atm4 = vw(81920, [128, 2, 4, CH], BF16)
    ket4 = vw(172032, [128, 4, 4, CH], BF16)
    o_fm = [vw(86016, [128, NH, CH], BF16), vw(90112, [128, NH, CH], BF16)]
    attn_t = [vw(94208, [128, D], F32)]
    wo_sb = vw(102400, [128, NDCH, D], BF16)

    xng_m = [vw(0, [128, NDCH, PW], BF16), vw(16384, [128, NDCH, PW], BF16)]
    guw = [vw(32768 + i * 24576, [128, NDCH, IF], BF16) for i in range(4)]
    sg_b = vw(131072, [128, NIC, PW], BF16)
    h_b = vw(137280, [128, NIC, PW], BF16)
    w_bc = [vw(143488, [128, TM], BF16), vw(151680, [128, TM], BF16)]

    NMH = NM // 2
    acc = vw(0, [128, NMH, D], F32)
    dwb = [vw(NMH * D * 4 + i * (NIC * D * 2), [128, NIC, D], BF16)
           for i in range(2 * EG)]
    _dw_end = NMH * D * 4 + 2 * EG * NIC * D * 2
    h_in = [vw(_dw_end + i * (NIC * CH * 2), [128, NIC, CH], BF16)
            for i in range(4 * EG)]

    c0 = s.add("pool", lambda g: g.memset(cst[:, 0:1], 1e-5))
    s.add("pool", lambda g: g.memset(cst[:, 1:2], 1.0))
    s.add("pool", lambda g: g.memset(cst[:, 2:3], 1.0 / 16.0))
    s.add("pool", lambda g: g.memset(cst[:, 3:4], -1.0 / 16.0))
    s.add("pool", lambda g: g.memset(cst[:, 4:5], -0.5 * math.log(DK)))
    s.add("pool", lambda g: g.memset(ones_f[:], 1.0))
    init_zs = s.add("pool", lambda g: g.memset(S_a[:], 0.0))
    l_flag = s.add("sp", lambda g: g.dma_start(out=cst[:, 5:6], in_=flag_d), dsem="ini")
    l_id = s.add("pool", lambda g: g.dma_start(out=ident_bf[:], in_=ident_d), dsem="in_id")
    l_mask = s.add("pool", lambda g: g.dma_start(out=mask_bf[:], in_=mask_d), dsem="in_mk")
    l_sel = s.add("pool", lambda g: g.dma_start(
        out=sel_sb[:], in_=sel_d.rearrange("p (e t) -> p e t", e=NE)), dsem="in_sel")
    l_rw = s.add("pool", lambda g: g.dma_start(
        out=rw_sb[:], in_=rwt_d.rearrange("(c p) e -> p c e", p=128)), dsem="in_rw")
    init_c = s.add("pool", lambda g: g.memset(dummy[:], 0.0), deps=[c0, init_zs])

    if stop_after == "init":
        return _stop_emit()
    NTC = (TP + TM) // 128
    xf_gate = [None] * 4
    xng_gate = [None] * 4
    xn_tm_gate = [None] * 4
    slot_evac = [None] * 4
    a0_group_write = [None] * NTG
    for tc in range(NTC):
        sl = tc % 4
        grp = tc // 4
        gsl = grp % 4
        src_ap = xp[tc * 128:(tc + 1) * 128, :] if tc < NP else \
            xm[(tc - NP) * 128:(tc - NP + 1) * 128, :]
        lx = s.add("sp", lambda g, src_ap=src_ap, sl=sl: g.dma_start(
            out=xf[sl][:], in_=src_ap), deps=[xf_gate[sl]], dsem=f"a0x{sl}")
        m_sum = s.add("dve", lambda v, sl=sl: v.reduce_sum(
            lns[:, sl, 0:1], xf[sl][:], axis=AX.X), deps=[lx])
        m_sq = s.add("act", lambda a, sl=sl: a.activation(
            sqd[:], xf[sl][:], AF.Square, accum_out=lns[:, sl, 1:2]),
            deps=[lx, m_sum])
        m_mean = s.add("dve", lambda v, sl=sl: v.tensor_scalar(
            lns[:, sl, 2:3], lns[:, sl, 0:1], 1.0 / D, None, op0=ALU.mult))
        m_m2 = s.add("act", lambda a, sl=sl: a.activation(
            lns[:, sl, 3:4], lns[:, sl, 2:3], AF.Square), deps=[m_mean])
        m_t = s.add("dve", lambda v, sl=sl: v.tensor_scalar(
            lns[:, sl, 4:5], lns[:, sl, 1:2], 1.0 / D, None, op0=ALU.mult),
            deps=[m_sq])
        m_var = s.add("dve", lambda v, sl=sl: v.tensor_sub(
            lns[:, sl, 5:6], lns[:, sl, 4:5], lns[:, sl, 3:4]), deps=[m_m2])
        m_srt = s.add("act", lambda a, sl=sl: a.activation(
            lns[:, sl, 6:7], lns[:, sl, 5:6], AF.Sqrt, bias=cst[:, 0:1]),
            deps=[m_var, init_c])
        m_rstd = s.add("dve", lambda v, sl=sl: v.reciprocal(
            lns[:, sl, 7:8], lns[:, sl, 6:7]), deps=[m_srt])
        m_xn = s.add("dve", lambda v, sl=sl: v.tensor_scalar(
            xn_tm[sl][:], xf[sl][:], lns[:, sl, 2:3], lns[:, sl, 7:8],
            op0=ALU.subtract, op1=ALU.mult), deps=[xn_tm_gate[sl]])
        xf_gate[sl] = m_xn
        tws = []
        tci = tc % 4
        for bq in range(4):
            bslot = (tc * 4 + bq) % 2
            area = bank_t[:, bslot * 512:bslot * 512 + 512]
            last_tp = None
            for j in range(4):
                dch = bq * 4 + j
                def tfn(pe, sl=sl, dch=dch, area=area, j=j):
                    return pe.transpose(
                        area[:, j * 128:(j + 1) * 128],
                        xn_tm[sl][:, dch * 128:(dch + 1) * 128], ident_bf[:])
                last_tp = s.add("pe", tfn,
                                deps=[m_xn, l_id, slot_evac[0], slot_evac[1],
                                      last_tp])
            e_t = s.add("act", lambda a, gsl=gsl, bq=bq, area=area, tci=tci: a.copy(
                xng[gsl][:, bq * 4:(bq + 1) * 4, tci * 128:(tci + 1) * 128],
                area[:]),
                deps=[last_tp, xng_gate[gsl] if tci == 0 else None])
            slot_evac[bslot] = e_t
            tws.append(e_t)
            if bq == 3:
                xn_tm_gate[sl] = last_tp
        if tci == 3:
            wxn = s.add("sp", lambda g, gsl=gsl, grp=grp: g.dma_start(
                out=xn_scr[grp, :, :],
                in_=xng[gsl][:].rearrange("p a b -> p (a b)")), deps=tws,
                dsem=f"a0w{grp % 2}")
            xng_gate[gsl] = wxn
            a0_group_write[grp] = wxn

    xng_rd_gate = list(xng_gate)
    xng_pe_gate = [None, None, None, None]
    wbuf_pe_gate = [None, None]
    bank_rd = [None] * 7
    bank_wr = [None] * 7
    ev_gate = [None] * 4
    gz_dve_gate = None
    sq_dve_gate = None
    scr_w_last = {}

    def a_load_xng(tg, sl):
        dep = [xng_rd_gate[sl], xng_pe_gate[sl],
               a0_group_write[-1], a0_group_write[-2]]
        h = s.add("sp", lambda g, tg=tg, sl=sl: g.dma_start(
            out=xng[sl][:].rearrange("p a b -> p (a b)"), in_=xn_scr[tg, :, :]),
            deps=dep, dsem=f"apx{sl}")
        xng_rd_gate[sl] = h
        return h

    def a_load_w(wsrc, qrt, slot):
        last = None
        for dch in range(NDCH):
            last = s.add("pool", lambda g, wsrc=wsrc, qrt=qrt, slot=slot, dch=dch:
                         g.dma_start(out=wbuf[slot][:, dch, :],
                                     in_=wsrc[dch * 128:(dch + 1) * 128,
                                              qrt * 512:(qrt + 1) * 512]),
                         deps=[wbuf_pe_gate[slot]] if dch == 0 else [],
                         dsem=f"apw{slot}")
        return last

    pass_list = [("k", wk_d, k_scr), ("g", wg_d, g_scr), ("v", wv_d, v_scr),
                 ("q", wq_d, q_scr)]
    bank_i = 0
    for pi, (pname, wsrc, scr) in enumerate(pass_list):
        ntg = NTGM if pname == "q" else NTG
        tg_off = TP // PW if pname == "q" else 0
        if stop_after == "pass_" + pname:
            return _stop_emit()
        for qrt in range(4):
            wslot = (pi * 4 + qrt) % 2
            lw = a_load_w(wsrc, qrt, wslot)
            w_readers = []
            for tg in range(ntg):
                xsl = tg % 4
                lx = a_load_xng(tg + tg_off, xsl)
                x_readers = []
                for fc in range(4):
                    bk = bank_i % 6
                    bank_i += 1
                    if pname != "v":
                        def mmf(pe, fc=fc, xsl=xsl, bk=bk, wslot=wslot):
                            r = None
                            for dch in range(NDCH):
                                r = pe.matmul(bank[bk][:],
                                              wbuf[wslot][:, dch, fc * 128:(fc + 1) * 128],
                                              xng[xsl][:, dch, :],
                                              start=(dch == 0), stop=(dch == NDCH - 1))
                            return r
                    else:
                        tci = fc
                        def mmf(pe, tci=tci, xsl=xsl, bk=bk, wslot=wslot):
                            r = None
                            for dch in range(NDCH):
                                r = pe.matmul(bank[bk][:],
                                              xng[xsl][:, dch, tci * 128:(tci + 1) * 128],
                                              wbuf[wslot][:, dch, :],
                                              start=(dch == 0), stop=(dch == NDCH - 1))
                            return r
                    mm = s.add("pe", mmf, deps=[lw, lx, bank_rd[bk]])
                    bank_wr[bk] = mm
                    x_readers.append(mm)
                    w_readers.append(mm)
                    evslot = (tg * 4 + fc) % 4
                    if pname == "g":
                        mop = s.add("dve", lambda v, bk=bk: v.tensor_scalar(
                            gzmin[:], bank[bk][:], 0.0, None, op0=ALU.min),
                            deps=[mm])
                        aop = s.add("act", lambda a, bk=bk: a.activation(
                            gtmp[:], bank[bk][:], AF.Abs), deps=[mm, mop, sq_dve_gate])
                        eop = s.add("act", lambda a: a.activation(
                            gtmp[:], gtmp[:], AF.Exp, scale=-1.0))
                        lop = s.add("act", lambda a: a.activation(
                            gtmp[:], gtmp[:], AF.Ln, bias=cst[:, 1:2]), deps=[init_c])
                        sub = s.add("dve", lambda v, evslot=evslot: v.tensor_sub(
                            ev[evslot][:], gzmin[:], gtmp[:]),
                            deps=[lop, ev_gate[evslot]])
                        sq_dve_gate = sub
                        bank_rd[bk] = sub
                        evh = sub
                    else:
                        evh = s.add("act", lambda a, bk=bk, evslot=evslot: a.copy(
                            ev[evslot][:], bank[bk][:]),
                            deps=[mm, ev_gate[evslot]])
                        bank_rd[bk] = evh
                    if pname != "v":
                        fcg = qrt * 4 + fc
                        wlast = None
                        for j in range(4):
                            cidx = (tg * PW + j * 128) // CH
                            def wfn(g, scr=scr, cidx=cidx, fcg=fcg, j=j, evslot=evslot):
                                return g.dma_start(
                                    out=scr[cidx, :, fcg * CH:(fcg + 1) * CH],
                                    in_=ev[evslot][:, j * 128:(j + 1) * 128])
                            wlast = s.add("act", wfn, deps=[evh],
                                          dsem=f"apo{evslot}")
                        ev_gate[evslot] = wlast
                    else:
                        tci = fc
                        cidx = tg * 4 + tci
                        fcol = qrt * 512
                        wlast = s.add("act", lambda g, cidx=cidx, fcol=fcol, evslot=evslot:
                                      g.dma_start(out=v_scr[cidx, :, fcol:fcol + 512],
                                                  in_=ev[evslot][:]),
                                      deps=[evh], dsem=f"apo{evslot}")
                        ev_gate[evslot] = wlast
                xng_pe_gate[xsl] = x_readers[-1]
            wbuf_pe_gate[wslot] = w_readers[-1]
    aproj_fin = [h for h in ev_gate if h is not None]

    if stop_after == "aproj":
        return _stop_emit()
    if stop_after == "aproj":
        return _stop_emit()
    gla = dict(
        k_gate=[None, None],
        v_gate=[None, None, None],
        act_last=None,
        pe_gate=[None, None],
        atm_gate=[None, None],
        ket_gate=[None] * 4,
        state_dep=init_zs,
        sbf_gate=None,
        tiny_gate=[None, None],
        tp_gate=[None] * 4,
        o_ev_last=None,
        ofm_pe_gate=None,
    )

    def gla_stage1(ci, main):
        sl = ci % 2
        lk = s.add("sp", lambda g, ci=ci, sl=sl: g.dma_start(
            out=kt_l[sl][:].rearrange("p a b -> p (a b)"), in_=k_scr[ci, :, :]),
            deps=[gla["k_gate"][sl]], dsem=f"glk{sl}")
        lg = s.add("sp", lambda g, ci=ci, sl=sl: g.dma_start(
            out=gt_l[sl][:].rearrange("p a b -> p (a b)"), in_=g_scr[ci, :, :]),
            deps=[gla["k_gate"][sl]], dsem=f"glg{sl}")
        vsl = ci % 3
        lv = s.add("sp", lambda g, ci=ci, vsl=vsl: g.dma_start(
            out=vt_l[vsl][:], in_=v_scr[ci, :, :]), deps=[gla["v_gate"][vsl]],
            dsem=f"glv{vsl}")
        lq = None
        if main:
            lq = s.add("sp", lambda g, ci=ci, sl=sl: g.dma_start(
                out=qt_l[sl][:].rearrange("p a b -> p (a b)"),
                in_=q_scr[ci - NP, :, :]), deps=[gla["k_gate"][sl]], dsem=f"glq{sl}")
        sc_last = None
        for h in range(NH):
            sc_last = s.add("dve", lambda v, h=h, sl=sl: v.tensor_tensor_scan(
                cum_b[:, h, :], gt_l[sl][:, h, :], ones_f[:, 0:CH], 0.0,
                ALU.add, ALU.mult),
                deps=[lg, init_c] + ([gla["act_last"]] if h == 0 else []))
        cumc = s.add("dve", lambda v, sl=sl: v.tensor_scalar(
            tiny[:, sl, :, 0:1], cum_b[:, :, CH - 1:CH], 1.0 / 16.0, None,
            op0=ALU.mult), deps=[sc_last, gla["tiny_gate"][sl]])
        lc = s.add("act", lambda a, sl=sl: a.activation(
            tiny[:, sl, :, 1:2], tiny[:, sl, :, 0:1], AF.Exp), deps=[cumc])
        kd = s.add("act", lambda a, sl=sl: a.activation(
            kd_all[sl][:].rearrange("p a b -> p (a b)"),
            cum_b[:].rearrange("p a b -> p (a b)"), AF.Exp, scale=cst[:, 3:4]),
            deps=[sc_last, init_c])
        kt_op = s.add("dve", lambda v, sl=sl: v.tensor_mul(
            ktq_all[sl][:].rearrange("p a b -> p (a b)"),
            kt_l[sl][:].rearrange("p a b -> p (a b)"),
            kd_all[sl][:].rearrange("p a b -> p (a b)")),
            deps=[kd, lk, gla["pe_gate"][sl]])
        ke_op = s.add("dve", lambda v, sl=sl: v.tensor_mul(
            ke_all[sl][:], ktq_all[sl][:],
            tiny[:, sl, :, 1:2].to_broadcast([128, NH, CH])),
            deps=[kt_op, lc, gla["pe_gate"][sl]])
        qt_op = None
        lqe = None
        if main:
            lqe = s.add("act", lambda a, sl=sl: a.activation(
                lq_all[sl][:].rearrange("p a b -> p (a b)"),
                cum_b[:].rearrange("p a b -> p (a b)"), AF.Exp,
                scale=cst[:, 2:3], bias=cst[:, 4:5]), deps=[sc_last])
            qt_op = s.add("dve", lambda v, sl=sl: v.tensor_mul(
                qt_all[sl][:].rearrange("p a b -> p (a b)"),
                qt_l[sl][:].rearrange("p a b -> p (a b)"),
                lq_all[sl][:].rearrange("p a b -> p (a b)")),
                deps=[lqe, lq, gla["pe_gate"][sl]])
        gla["k_gate"][sl] = ke_op if not main else qt_op
        gla["act_last"] = lqe if main else kd
        return dict(sl=sl, vsl=vsl, lv=lv, lc=lc, kt_op=kt_op, ke_op=ke_op, qt_op=qt_op)

    def gla_stage2(st, main, S_cur, S_nxt):
        sl = st["sl"]
        vsl = st["vsl"]
        lv = st["lv"]
        sbf = None
        if main:
            sbf = s.add("act", lambda a: a.copy(S_bf[:], S_cur[:]),
                        deps=[gla["state_dep"], gla["sbf_gate"]])
        t1 = s.add("dve", lambda v, sl=sl: v.tensor_mul(
            S_nxt.rearrange("p (a b) -> p a b", a=NH),
            S_cur.rearrange("p (a b) -> p a b", a=NH),
            tiny[:, sl, :, 1:2].to_broadcast([128, NH, CH])),
            deps=[st["lc"], gla["state_dep"]] + ([sbf] if sbf else []))
        last_dve = t1
        o_ev = None
        o_grp = None
        p_last = None
        for g4 in range(4):
            gsl = g4 % 2
            bkA, bkO, bkP = gsl, 2 + gsl, 4 + gsl
            hs = range(g4 * 4, g4 * 4 + 4)
            if main:
                def atg(pe, hs=hs, bkA=bkA, sl=sl):
                    r = None
                    for j, h in enumerate(hs):
                        r = pe.matmul(bank[bkA][:, j * 128:(j + 1) * 128],
                                      ktq_all[sl][:, h, :], qt_all[sl][:, h, :],
                                      start=True, stop=True)
                    return r
                at_grp = s.add("pe", atg,
                               deps=[st["kt_op"], st["qt_op"], bank_rd[bkA]])
                atm = s.add("dve", lambda v, gsl=gsl, bkA=bkA: v.tensor_mul(
                    atm4[:, gsl, :, :], bank[bkA][:], mask_bf[:]),
                    deps=[at_grp, l_mask, gla["atm_gate"][gsl]])
                bank_rd[bkA] = atm
                def og(pe, hs=hs, gsl=gsl, bkO=bkO, sl=sl, vsl=vsl):
                    r = None
                    for j, h in enumerate(hs):
                        pe.matmul(bank[bkO][:, j * 128:(j + 1) * 128],
                                  vt_l[vsl][:, h * 128:(h + 1) * 128],
                                  atm4[:, gsl, j, :], start=True, stop=False)
                        r = pe.matmul(bank[bkO][:, j * 128:(j + 1) * 128],
                                      S_bf[:, h * DK:(h + 1) * DK],
                                      qt_all[sl][:, h, :], start=False, stop=True)
                    return r
                o_grp = s.add("pe", og, deps=[atm, lv, sbf, bank_rd[bkO]])
                gla["atm_gate"][gsl] = o_grp
                o_ev = s.add("act", lambda a, g4=g4, bkO=bkO, sl=sl: a.copy(
                    o_fm[sl][:, g4 * 4:(g4 + 1) * 4, :], bank[bkO][:]),
                    deps=[o_grp, gla["ofm_pe_gate"]])
                bank_rd[bkO] = o_ev
                gla["o_ev_last"] = o_ev
            base = bank_t if g4 < 2 else bank6_bf
            tp_area = base[:, (g4 % 2) * 512:(g4 % 2) * 512 + 512]
            def tpg(pe, hs=hs, sl=sl, tp_area=tp_area):
                r = None
                for j, h in enumerate(hs):
                    r = pe.transpose(
                        tp_area[:, j * 128:(j + 1) * 128],
                        ke_all[sl][:, h, :], ident_bf[:])
                return r
            tp_grp = s.add("pe", tpg,
                           deps=[st["ke_op"], gla["tp_gate"][g4], l_id])
            ket_ev = s.add("act", lambda a, g4=g4, tp_area=tp_area: a.copy(
                ket4[:, g4, :, :], tp_area[:]),
                deps=[tp_grp, gla["ket_gate"][g4]])
            gla["tp_gate"][g4] = ket_ev
            def pg(pe, hs=hs, g4=g4, bkP=bkP, vsl=vsl):
                r = None
                for j, h in enumerate(hs):
                    r = pe.matmul(bank[bkP][:, j * 128:(j + 1) * 128],
                                  ket4[:, g4, j, :],
                                  vt_l[vsl][:, h * 128:(h + 1) * 128],
                                  start=True, stop=True)
                return r
            p_grp = s.add("pe", pg, deps=[ket_ev, lv, bank_rd[bkP]])
            gla["ket_gate"][g4] = p_grp
            p_add = s.add("dve", lambda v, g4=g4, bkP=bkP: v.tensor_add(
                S_nxt[:, g4 * 512:(g4 + 1) * 512],
                S_nxt[:, g4 * 512:(g4 + 1) * 512], bank[bkP][:]),
                deps=[p_grp, t1])
            bank_rd[bkP] = p_add
            last_dve = p_add
            p_last = p_grp
        gla["v_gate"][vsl] = p_last
        gla["pe_gate"][sl] = p_last
        gla["tiny_gate"][sl] = last_dve
        gla["state_dep"] = last_dve
        if main:
            gla["sbf_gate"] = o_grp
        return last_dve

    bar_g = s.barrier()
    lwo = None
    for dch in range(NDCH):
        lwo = s.add("pool", lambda g, dch=dch: g.dma_start(
            out=wo_sb[:, dch, :], in_=wo_d[dch * 128:(dch + 1) * 128, :]),
            deps=[bar_g] if dch == 0 else [], dsem="wo")

    gla["k_gate"] = [bar_g, bar_g]
    gla["v_gate"] = [bar_g, bar_g, bar_g]
    gla["tp_gate"] = [bar_g] * 4
    gla["pe_gate"] = [bar_g, bar_g]
    gla["atm_gate"] = [bar_g, bar_g]
    gla["ket_gate"] = [bar_g] * 4
    for _bk in range(7):
        bank_rd[_bk] = bar_g

    gla["tiny_gate"] = [bar_g, bar_g]
    st_pipe = gla_stage1(0, False)
    for c in range(NP):
        st_cur = st_pipe
        if c + 1 < NP:
            st_pipe = gla_stage1(c + 1, False)
        S_cur = S_a if c % 2 == 0 else S_b
        S_nxt = S_b if c % 2 == 0 else S_a
        gla_stage2(st_cur, False, S_cur, S_nxt)
    S_fin = S_a if NP % 2 == 0 else S_b
    S_oth = S_b if NP % 2 == 0 else S_a
    f_op = s.add("dve", lambda v: v.tensor_scalar(
        S_fin[:], S_fin[:], cst[:, 5:6], None, op0=ALU.mult),
        deps=[gla["state_dep"], l_flag])
    gla["state_dep"] = f_op

    if stop_after == "p1":
        return _stop_emit()
    wo_banks = [0, 1, 2, 3]
    attn_done = []
    attn_gate = [None, None]
    wo_pe_gate = [None, None]
    st_pipe = gla_stage1(NP, True)
    for c in range(NM):
        ci = NP + c
        st_cur = st_pipe
        if c + 1 < NM:
            st_pipe = gla_stage1(ci + 1, True)
        S_cur = S_fin if c % 2 == 0 else S_oth
        S_nxt = S_oth if c % 2 == 0 else S_fin
        gla["ofm_pe_gate"] = wo_pe_gate[c % 2]
        gla_stage2(st_cur, True, S_cur, S_nxt)
        asl = 0
        ev_deps = []
        wo_mm_last = None
        for dmg in range(4):
            bk = wo_banks[dmg]
            csl = ci % 2
            def wmm(pe, dmg=dmg, bk=bk, csl=csl):
                r = None
                for fcc in range(NDCH):
                    r = pe.matmul(bank[bk][:], o_fm[csl][:, fcc, :],
                                  wo_sb[:, fcc, dmg * 512:(dmg + 1) * 512],
                                  start=(fcc == 0), stop=(fcc == NDCH - 1))
                return r
            mm = s.add("pe", wmm, deps=[gla["o_ev_last"], lwo, bank_rd[bk]])
            e1 = s.add("dve", lambda v, dmg=dmg, bk=bk, asl=asl: v.tensor_copy(
                attn_t[asl][:, dmg * 512:(dmg + 1) * 512], bank[bk][:]),
                deps=[mm, attn_gate[asl]])
            bank_rd[bk] = e1
            ev_deps.append(e1)
            wo_mm_last = mm
        wo_pe_gate[c % 2] = wo_mm_last
        wr = s.add("pool", lambda g, c=c, asl=asl: g.dma_start(
            out=at_scr[c, :, :], in_=attn_t[asl][:]), deps=ev_deps, dsem=f"nat{asl}")
        attn_gate[asl] = wr
        attn_done.append(wr)

    if stop_after == "p3":
        return _stop_emit()
    bar_m1 = s.barrier()
    for _bk in range(7):
        bank_rd[_bk] = bar_m1
    xng_rd_gate2 = [bar_m1, bar_m1]
    xng_pe2 = [bar_m1, bar_m1]
    tp_gate2 = bar_m1
    w_all_done = None
    for tc in range(NM):
        xsl = tc % 2
        gidx = (TP + tc * 128) // PW
        tci = tc % 4
        lx = s.add("sp", lambda g, gidx=gidx, tci=tci, xsl=xsl: g.dma_start(
            out=xng_m[xsl][:, :, 0:128],
            in_=xn_scr[gidx, :, :].rearrange("p (c t) -> p c t", t=PW)[
                :, :, tci * 128:(tci + 1) * 128]),
            deps=[xng_rd_gate2[xsl], xng_pe2[xsl]], dsem=f"apx{xsl}")
        xng_rd_gate2[xsl] = lx
        bk = 6
        def rmm(pe, xsl=xsl, bk=bk):
            r = None
            for dch in range(NDCH):
                r = pe.matmul(bank[bk][:, 0:NE], xng_m[xsl][:, dch, 0:128],
                              rw_sb[:, dch, :], start=(dch == 0),
                              stop=(dch == NDCH - 1))
            return r
        mm = s.add("pe", rmm, deps=[lx, l_rw, bank_rd[bk]])
        xng_pe2[xsl] = mm
        e1 = s.add("dve", lambda v, bk=bk: v.tensor_copy(logit_t[:], bank[bk][:, 0:NE]),
                   deps=[mm])
        bank_rd[bk] = e1
        m8 = s.add("dve", lambda v: v.max(out=top8[:], in_=logit_t[:]))
        nb = s.add("dve", lambda v: v.tensor_scalar(
            rtiny[:, 0:1], top8[:, 0:1], -1.0, None, op0=ALU.mult))
        ex = s.add("act", lambda a: a.activation(
            wexp[:], logit_t[:], AF.Exp, bias=rtiny[:, 0:1]), deps=[nb])
        msk = s.add("dve", lambda v: v.tensor_scalar(
            rmask[:], logit_t[:], top8[:, 3:4], None, op0=ALU.is_ge), deps=[ex])
        wmul = s.add("dve", lambda v: v.tensor_mul(wexp[:], wexp[:], rmask[:]),
                     deps=[ex])
        wsum = s.add("dve", lambda v: v.reduce_sum(rtiny[:, 1:2], wexp[:], axis=AX.X))
        wrec = s.add("dve", lambda v: v.reciprocal(rtiny[:, 0:1], rtiny[:, 1:2]),
                     deps=[ex])
        wnrm = s.add("dve", lambda v: v.tensor_scalar(
            wexp_bf[:], wexp[:], rtiny[:, 0:1], None, op0=ALU.mult))
        tp = s.add("pe", lambda pe: pe.transpose(
            bank_t[0:NE, 512:640], wexp_bf[:], ident_bf[:]),
            deps=[wnrm, tp_gate2, l_id])
        ev1 = s.add("act", lambda a, tc=tc: a.copy(
            w_all[0:NE, tc * 128:(tc + 1) * 128], bank_t[0:NE, 512:640]),
            deps=[tp])
        tp_gate2 = ev1
        w_all_done = ev1

    if stop_after == "router":
        return _stop_emit()
    guw_pe_gate = [bar_m1] * 4
    wbc_dve_gate = [None, None]
    wbc_ev = [[], []]
    sg_dve_gate = [None] * NIC
    hb_wr_gate = [None] * NIC
    m1_fin = None

    def emit_wbc_mm(e, ng):
        slot = e % 2
        bk = 6
        mm = s.add("pe", lambda pe, e=e, ng=ng, bk=bk: pe.matmul(
            bank[bk][:], sel_sb[:, e, :], w_all[:, ng * PW:(ng + 1) * PW],
            start=True, stop=True),
            deps=[w_all_done, l_sel, bank_rd[bk], wbc_dve_gate[slot]])
        e1 = s.add("act", lambda a, ng=ng, bk=bk, slot=slot: a.copy(
            w_bc[slot][:, ng * PW:(ng + 1) * PW], bank[bk][:]),
            deps=[mm])
        bank_rd[bk] = e1
        wbc_ev[slot].append(e1)

    for ng in range(TM // PW):
        emit_wbc_mm(0, ng)

    for e in range(NE):
        gslot, uslot = e % 2, 2 + e % 2
        lgw = s.add("pool", lambda g, e=e, gslot=gslot: g.dma_start(
            out=guw[gslot][:],
            in_=gwt_d[e].rearrange("(c p) i -> p c i", p=128)),
            deps=[guw_pe_gate[gslot]], dsem=f"gw{gslot}")
        luw = s.add("pool", lambda g, e=e, uslot=uslot: g.dma_start(
            out=guw[uslot][:],
            in_=uwt_d[e].rearrange("(c p) i -> p c i", p=128)),
            deps=[guw_pe_gate[uslot]], dsem=f"uw{uslot - 2}")
        bc_mine = list(wbc_ev[e % 2])
        wbc_ev[(e + 1) % 2] = []
        nxt = [(e + 1, ng) for ng in range(TM // PW)] if e + 1 < NE else []
        gate_sil = {}
        last_pe_of_e = None

        def m1_load_xn(tg):
            xsl = tg % 2
            h = s.add("sp", lambda g, tg=tg, xsl=xsl: g.dma_start(
                out=xng_m[xsl][:].rearrange("p a b -> p (a b)"),
                in_=xn_scr[TP // PW + tg, :, :]),
                deps=[xng_rd_gate2[xsl], xng_pe2[xsl]], dsem=f"apx{xsl}")
            xng_rd_gate2[xsl] = h
            return h

        if e == 0:
            m1_lx = {0: m1_load_xn(0)}
        for tg in range(TM // PW):
            xsl = tg % 2
            lx = m1_lx.pop(tg) if tg in m1_lx else m1_load_xn(tg)
            for ic in range(NIC):
                bk = ic % 3
                def gmm(pe, ic=ic, xsl=xsl, bk=bk, gslot=gslot):
                    r = None
                    for dch in range(NDCH):
                        r = pe.matmul(bank[bk][:],
                                      guw[gslot][:, dch, ic * 128:(ic + 1) * 128],
                                      xng_m[xsl][:, dch, :],
                                      start=(dch == 0), stop=(dch == NDCH - 1))
                    return r
                mm = s.add("pe", gmm, deps=[lgw, lx, bank_rd[bk]])
                if nxt:
                    emit_wbc_mm(*nxt.pop(0))
                sil = s.add("act", lambda a, ic=ic, bk=bk: a.activation(
                    sg_b[:, ic, :], bank[bk][:], AF.Silu),
                    deps=[mm, sg_dve_gate[ic]])
                bank_rd[bk] = sil
                gate_sil[ic] = sil
            nxt_tg = tg + 1
            if nxt_tg < TM // PW and nxt_tg not in m1_lx:
                m1_lx[nxt_tg] = m1_load_xn(nxt_tg)
            for ic in range(NIC):
                bk = 3 + ic % 3
                def umm(pe, ic=ic, xsl=xsl, bk=bk, uslot=uslot):
                    r = None
                    for dch in range(NDCH):
                        r = pe.matmul(bank[bk][:],
                                      guw[uslot][:, dch, ic * 128:(ic + 1) * 128],
                                      xng_m[xsl][:, dch, :],
                                      start=(dch == 0), stop=(dch == NDCH - 1))
                    return r
                mm = s.add("pe", umm, deps=[luw, lx, bank_rd[bk]])
                if nxt:
                    emit_wbc_mm(*nxt.pop(0))
                h1 = s.add("dve", lambda v, ic=ic, bk=bk: v.tensor_mul(
                    h_b[:, ic, :], sg_b[:, ic, :], bank[bk][:]),
                    deps=[mm, gate_sil[ic], hb_wr_gate[ic]])
                h2 = s.add("dve", lambda v, ic=ic, tg=tg, e=e: v.tensor_mul(
                    h_b[:, ic, :], h_b[:, ic, :],
                    w_bc[e % 2][:, tg * PW:(tg + 1) * PW]),
                    deps=bc_mine)
                sg_dve_gate[ic] = h1
                wbc_dve_gate[e % 2] = h2
                bank_rd[bk] = h1
                wl = None
                for j in range(4):
                    tt = tg * 4 + j
                    def hw(g, e=e, tt=tt, ic=ic, j=j):
                        return g.dma_start(
                            out=h_scr[e, tt, ic * 128:(ic + 1) * 128, :],
                            in_=h_b[:, ic, j * 128:(j + 1) * 128])
                    wl = s.add("sp", hw, deps=[h2], dsem=f"hw{ic}")
                hb_wr_gate[ic] = wl
                m1_fin = wl
                last_pe_of_e = mm
            xng_pe2[xsl] = last_pe_of_e
            if tg == TM // PW - 1 and e + 1 < NE:
                m1_lx = {0: m1_load_xn(0)}
        guw_pe_gate[gslot] = last_pe_of_e
        guw_pe_gate[uslot] = last_pe_of_e

    if stop_after == "m1":
        return _stop_emit()
    m1_join = s.add("pool", lambda g: g.memset(dummy[:], 0.0), deps=hb_wr_gate)
    attn_join = s.add("pool", lambda g: g.memset(dummy[:], 0.0),
                      deps=[attn_done[-1], attn_done[-2]])
    bar_m2 = s.barrier()

    NMH = NM // 2
    NG = NE // EG
    out_done = []
    dw_gate = [bar_m2] * (2 * EG)
    hin_gate = [bar_m2] * (4 * EG)
    for bk in range(7):
        bank_rd[bk] = bar_m2
    half_gate = bar_m2
    for hh in range(2):
        a1s = []
        for tl in range(NMH):
            tt = hh * NMH + tl
            a1s.append(s.add("act", lambda a, tt=tt, tl=tl: a.dma_start(
                out=acc[:, tl, :], in_=xm[tt * 128:(tt + 1) * 128, :]),
                deps=[half_gate], dsem=f"mxi{tl}"))
        acc_init = []
        for tl in range(NMH):
            tt = hh * NMH + tl
            acc_init.append(s.add("pool", lambda g, tt=tt, tl=tl: g.dma_start(
                out=acc[:, tl, :], in_=at_scr[tt, :, :], accum_op=ALU.add),
                deps=[a1s[tl], attn_join, m1_join], dsem=f"mai{tl}"))
        acc_last = [None] * NMH
        for grp in range(NG):
            dws = []
            for ei in range(EG):
                e = grp * EG + ei
                slot = (grp % 2) * EG + ei
                last = s.add("pool", lambda g, e=e, slot=slot: g.dma_start(
                    out=dwb[slot][:],
                    in_=dwt_d[e].rearrange("(c p) m -> p c m", p=128)),
                    deps=[dw_gate[slot]], dsem=f"dw{slot}")
                dws.append(last)
            for tl in range(NMH):
                tt = hh * NMH + tl
                hins = []
                hslots = []
                for ei in range(EG):
                    e = grp * EG + ei
                    slot = (tl % 4) * EG + ei
                    hop = s.add("sp", lambda g, e=e, tt=tt, slot=slot: g.dma_start(
                        out=h_in[slot][:],
                        in_=h_scr[e, tt, :, :].rearrange("(c p) t -> p c t", p=128)),
                        deps=[hin_gate[slot], m1_join if grp == 0 else None],
                        dsem=f"hi{slot}")
                    hins.append(hop)
                    hslots.append(slot)
                last_mm = None
                for dmg in range(4):
                    bk = (tl * 4 + dmg) % 6
                    def dmm(pe, tl=tl, dmg=dmg, bk=bk, grp=grp):
                        r = None
                        first = True
                        for ei in range(EG):
                            slot = (tl % 4) * EG + ei
                            for ic in range(NIC):
                                r = pe.matmul(
                                    bank[bk][:], h_in[slot][:, ic, :],
                                    dwb[(grp % 2) * EG + ei][:, ic,
                                                             dmg * 512:(dmg + 1) * 512],
                                    start=first, stop=(ei == EG - 1 and ic == NIC - 1))
                                first = False
                        return r
                    mm = s.add("pe", dmm, deps=hins + dws + [bank_rd[bk]])
                    ad = s.add("dve", lambda v, tl=tl, dmg=dmg, bk=bk: v.tensor_add(
                        acc[:, tl, dmg * 512:(dmg + 1) * 512],
                        acc[:, tl, dmg * 512:(dmg + 1) * 512], bank[bk][:]),
                        deps=[mm, acc_last[tl], acc_init[tl]])
                    bank_rd[bk] = ad
                    acc_last[tl] = ad
                    last_mm = mm
                for sl2 in hslots:
                    hin_gate[sl2] = last_mm
                for ei in range(EG):
                    dw_gate[(grp % 2) * EG + ei] = last_mm
        half_outs = []
        for tl in range(NMH):
            tt = hh * NMH + tl
            wop = s.add("sp", lambda g, tt=tt, tl=tl: g.dma_start(
                out=out_d[tt * 128:(tt + 1) * 128, :], in_=acc[:, tl, :]),
                deps=[acc_last[tl]], dsem="outw")
            half_outs.append(wop)
            out_done.append(wop)
        half_gate = half_outs[-1]
    s.add("sp", lambda g: g.nop(), deps=out_done)

    built = s.build()
    for cm in reversed(built):
        cm.__exit__(None, None, None)
    for cm in reversed(ctxs):
        cm.__exit__(None, None, None)
    return nc, s


def host_inputs(x, Wq, Wk, Wv, Wg, Wo, router_w, gate_w, up_w, down_w,
                NP=16, NM=16):
    x = np.ascontiguousarray(np.asarray(x, np.float32))
    B, S, _ = x.shape
    NE = np.asarray(gate_w).shape[0]
    rwt = np.ascontiguousarray(np.asarray(router_w, np.float32).T)
    gwt = np.ascontiguousarray(np.asarray(gate_w, np.float32).transpose(0, 2, 1))
    uwt = np.ascontiguousarray(np.asarray(up_w, np.float32).transpose(0, 2, 1))
    dwt = np.ascontiguousarray(np.asarray(down_w, np.float32).transpose(0, 2, 1))
    ident = np.eye(128, dtype=np.float32)
    maskst = np.ascontiguousarray(np.tile(np.triu(np.ones((128, 128), np.float32)), (1, 4)))
    sel = np.zeros((NE, NE, 128), np.float32)
    for e in range(NE):
        sel[e, e, :] = 1.0
    sel = np.ascontiguousarray(sel.reshape(NE, NE * 128))
    TP, TM = NP * CH, NM * CH
    n_half = S // TM
    in_maps = []
    n_cores = B * n_half
    for c in range(n_cores):
        b, m = c // n_half, c % n_half
        in_maps.append(dict(
            xp=np.ascontiguousarray(x[b, 0:TP]),
            xm=np.ascontiguousarray(x[b, TM * m:TM * (m + 1)]),
            wq=np.ascontiguousarray(np.asarray(Wq, np.float32)),
            wk=np.ascontiguousarray(np.asarray(Wk, np.float32)),
            wv=np.ascontiguousarray(np.asarray(Wv, np.float32)),
            wg=np.ascontiguousarray(np.asarray(Wg, np.float32)),
            wo=np.ascontiguousarray(np.asarray(Wo, np.float32)),
            rwt=rwt, gwt=gwt, uwt=uwt, dwt=dwt,
            flag=np.full((128, 1), float(min(m, 1)), np.float32),
            ident=ident, maskst=maskst, sel=sel,
        ))
    return in_maps


def kernel(L0_attn_x, L0_attn_Wq, L0_attn_Wk, L0_attn_Wv, L0_attn_Wg, L0_attn_Wo,
           L0_ffn_router_w, L0_ffn_gate_w, L0_ffn_up_w, L0_ffn_down_w):
    x = np.ascontiguousarray(np.asarray(L0_attn_x, np.float32))
    B, S, _ = x.shape
    NE = np.asarray(L0_ffn_gate_w).shape[0]
    nc, _ = build_program(NP=16, NM=16, NE=NE, EG=2)
    in_maps = host_inputs(x, L0_attn_Wq, L0_attn_Wk, L0_attn_Wv, L0_attn_Wg,
                          L0_attn_Wo, L0_ffn_router_w, L0_ffn_gate_w,
                          L0_ffn_up_w, L0_ffn_down_w)
    res = run_bass_kernel_spmd(nc, in_maps, core_ids=list(range(8)))
    outp = np.empty((B, S, D), np.float32)
    for c in range(8):
        b, m = c // 2, c % 2
        outp[b, 2048 * m:2048 * (m + 1)] = res.results[c]["out"]
    return outp
